# version 03
# baseline (speedup 1.0000x reference)
import math
import numpy as np
import ml_dtypes
from contextlib import ExitStack

import concourse.bass as bass
import concourse.mybir as mybir
from concourse.bass_utils import run_bass_kernel_spmd

F32 = mybir.dt.float32
BF16 = mybir.dt.bfloat16
AF = mybir.ActivationFunctionType
ALU = mybir.AluOpType
ts = bass.ts
BF = ml_dtypes.bfloat16

D = 4096
T = 4096
DH = 128
NQH = 16
NKV = 4
NT = 32
NG = 8
DC = 32
F_REAL = 11008
F_PAD = 11264
FT = 88
EPS = 1e-5
EXP_SCALE = 1.0 / math.sqrt(DH)
EXP_BIAS = -20.0
CTX_OFF = (0, 132, 512, 644)
ENGS = ("pe", "dve", "act", "pool", "sp")
PAIRS = [[0, 1], [2, 3], [4, 5], [6, 7]]


class Prog:

    def __init__(self, cnt=None):
        self.events = []
        self.cnt = dict(cnt or {})

    def c(self, sem):
        return self.cnt.get(sem, 0)

    def ev(self, eng, fn, waits=(), inc=None, by=1):
        if inc is None and eng == "dve":
            inc = "sdv"
        if inc is None and eng == "act":
            inc = "sac"
        waits = [(s, v) for (s, v) in waits if v > 0]
        if eng == "dve" and fn is not None and self.c("sdv") > 0:
            waits.append(("sdv", self.c("sdv")))
        if eng == "act" and fn is not None and self.c("sac") > 0:
            waits.append(("sac", self.c("sac")))
        self.events.append((eng, fn, waits, inc, by))
        if inc:
            self.cnt[inc] = self.cnt.get(inc, 0) + by
        return self.cnt.get(inc, 0) if inc else 0

    def pe(self, fn, waits=(), inc=None):
        return self.ev("pe", fn, waits, inc)

    def dma(self, eng, fn, waits=(), inc="err"):
        return self.ev(eng, fn, waits, inc, 16)

    def fence(self, eng, waits):
        self.ev(eng, None, waits, inc=False)


def replay(prog, block, sems):
    def body_for(name):
        def body(eng):
            seen = {}
            for e, fn, waits, inc, by in prog.events:
                if e != name:
                    continue
                for s, v in waits:
                    if seen.get(s, -1) >= v:
                        continue
                    eng.wait_ge(sems[s], v)
                    seen[s] = v
                if fn is None:
                    continue
                ins = fn(eng)
                if inc:
                    ins.then_inc(sems[inc], by)
        return body
    block.tensor(body_for("pe"))
    block.vector(body_for("dve"))
    block.scalar(body_for("act"))
    block.gpsimd(body_for("pool"))
    block.sync(body_for("sp"))


_ECACHE = {}

I32 = mybir.dt.int32
RSQRT_MAGIC = 0x5F3759DF


def emit_rsqrt(p, var_ap, dst_ap, tmp):
    v = tmp[:, 0:1]
    y = tmp[:, 1:2]
    a = tmp[:, 2:3]
    p.ev("dve", lambda e: e.tensor_scalar_add(v, var_ap, EPS))
    p.ev("dve", lambda e: e.tensor_scalar(
        y.bitcast(I32), v.bitcast(I32), 1, None, op0=ALU.arith_shift_right))
    p.ev("dve", lambda e: e.tensor_scalar(
        y.bitcast(I32), y.bitcast(I32), RSQRT_MAGIC, -1,
        op0=ALU.subtract, op1=ALU.mult))
    for it in range(3):
        out_y = dst_ap if it == 2 else y
        p.ev("dve", lambda e: e.tensor_mul(a, y, y))
        p.ev("dve", lambda e: e.tensor_mul(a, a, v))
        p.ev("dve", lambda e: e.tensor_scalar(a, a, -0.5, 1.5, op0=ALU.mult, op1=ALU.add))
        p.ev("dve", (lambda out_y=out_y: lambda e: e.tensor_mul(out_y, y, a))())
    return p.c("sdv")


def build_nc():
    _ECACHE.clear()
    nc = bass.Bass("TRN2", target_bir_lowering=False)

    B = {}
    B["x_d"] = nc.dram_tensor("x", [T, D], F32, kind="ExternalInput").ap()
    B["wq_d"] = nc.dram_tensor("wq", [D, NQH * DH], F32, kind="ExternalInput").ap()
    B["wk_d"] = nc.dram_tensor("wk", [D, NKV * DH], F32, kind="ExternalInput").ap()
    B["wv_d"] = nc.dram_tensor("wv", [D, NKV * DH], F32, kind="ExternalInput").ap()
    B["wo_d"] = nc.dram_tensor("wo", [D, D], F32, kind="ExternalInput").ap()
    B["wgf_d"] = nc.dram_tensor("wgf", [F_PAD, D], F32, kind="ExternalInput").ap()
    B["wuf_d"] = nc.dram_tensor("wuf", [F_PAD, D], F32, kind="ExternalInput").ap()
    B["wdf_d"] = nc.dram_tensor("wdf", [D, F_PAD], F32, kind="ExternalInput").ap()
    B["xo_d"] = nc.dram_tensor("xown", [2048, D], F32, kind="ExternalInput").ap()
    B["cos_d"] = nc.dram_tensor("costab", [DH, T], BF16, kind="ExternalInput").ap()
    B["sin_d"] = nc.dram_tensor("sintab", [DH, T], BF16, kind="ExternalInput").ap()
    B["msk_d"] = nc.dram_tensor("masks", [128, 896], BF16, kind="ExternalInput").ap()
    B["idb_d"] = nc.dram_tensor("identb", [128, 128], BF16, kind="ExternalInput").ap()
    B["rt_d"] = nc.dram_tensor("rotmat", [128, 128], BF16, kind="ExternalInput").ap()
    B["out_d"] = nc.dram_tensor("out", [2048, D], F32, kind="ExternalOutput").ap()

    B["ctx_shr"] = nc.dram_tensor("ctx_shr", [2 * T, NQH * DH], BF16,
                                  addr_space="Shared", kind="Internal").ap()
    B["scr_g"] = nc.dram_tensor("scr_g", [128, FT * DC * 128], BF16, kind="Internal").ap()
    B["scr_u"] = nc.dram_tensor("scr_u", [128, FT * DC * 128], BF16, kind="Internal").ap()
    B["scr_d"] = nc.dram_tensor("scr_d", [128, 4 * FT * 1024], BF16, kind="Internal").ap()
    B["b1i"] = nc.dram_tensor("b1i", [128, 8], F32, kind="Internal").ap()
    B["b1o"] = nc.dram_tensor("b1o", [128, 8], F32, kind="Internal").ap()

    sem_names = ["scs", "sz", "sx0", "sx1", "sco", "ssi",
                 "swk0", "swk1", "swv0", "swv1", "swq0", "swq1",
                 "scw", "sbw", "sbl0", "sbl1", "scc1",
                 "spe", "sdv", "sac", "sctx", "sxs", "s2r", "s2w0", "s2w1",
                 "s3g0", "s3g1", "srb0", "srb1", "srs", "s3w0", "s3w1", "syw"]

    with ExitStack() as outer:
        E = outer.enter_context
        B["consts"] = E(nc.sbuf_tensor("consts", [128, 8], F32))
        B["idb"] = E(nc.sbuf_tensor("idb", [128, 128], BF16))
        B["rt"] = E(nc.sbuf_tensor("rt", [128, 128], BF16))
        B["msk"] = E(nc.sbuf_tensor("msk", [128, 896], BF16))
        B["stats2"] = E(nc.sbuf_tensor("stats2", [128, 32], F32))
        B["zero8"] = E(nc.sbuf_tensor("zero8", [128, 8], F32))
        B["slin"] = [E(nc.sbuf_tensor(f"slin{i}", [128, 2048], BF16)) for i in range(2)]
        B["stg"] = [E(nc.sbuf_tensor("stg0", [128, 2048], BF16))] * 2
        sems = {n: E(nc.semaphore(n)) for n in sem_names}

        for j, v in enumerate([1.0, 0.0, EXP_BIAS, EXP_SCALE, EPS, -0.5]):
            nc.const_aps.register(F32, v, B["consts"][:, j:j + 1])

        units = make_units(B)

        with ExitStack() as st1:
            E1 = st1.enter_context
            B["K"] = E1(nc.sbuf_tensor("K", [128, NKV * T], BF16))
            B["Vh"] = E1(nc.sbuf_tensor("Vh", [128, NT * NKV * 132], BF16))
            B["LNXT"] = E1(nc.sbuf_tensor("LNXT", [128, DC * 512], BF16))
            B["QT"] = E1(nc.sbuf_tensor("QT", [128, NQH * 512], BF16))
            B["xbuf"] = [E1(nc.sbuf_tensor(f"xbuf{i}", [128, D], F32)) for i in range(2)]
            B["lnb"] = E1(nc.sbuf_tensor("lnb", [128, D], BF16))
            B["wqs"] = [E1(nc.sbuf_tensor(f"wqs{i}", [128, 2048], BF16)) for i in range(2)]
            B["wks"] = [E1(nc.sbuf_tensor(f"wks{i}", [128, 512], BF16)) for i in range(2)]
            B["wvs"] = [E1(nc.sbuf_tensor(f"wvs{i}", [128, 512], BF16)) for i in range(2)]
            B["cosg"] = E1(nc.sbuf_tensor("cosg", [128, 512], BF16))
            B["sing"] = E1(nc.sbuf_tensor("sing", [128, 512], BF16))
            B["praw"] = [E1(nc.sbuf_tensor(f"praw{i}", [128, 512], BF16)) for i in range(2)]
            B["t1b"] = [E1(nc.sbuf_tensor("t1b0", [128, 512], BF16))] * 2
            B["t2b"] = [E1(nc.sbuf_tensor("t2b0", [128, 512], BF16))] * 2
            B["Pt"] = [E1(nc.sbuf_tensor(f"Pt{i}", [128, 1024], BF16)) for i in range(2)]
            B["cstg"] = E1(nc.sbuf_tensor("cstg", [128, 4 * NQH * DH], BF16))
            B["bnb"] = E1(nc.sbuf_tensor("bnb", [128, 48], F32))
            B["agb"] = E1(nc.sbuf_tensor("agb", [128, 2], F32))
            B["rstd"] = E1(nc.sbuf_tensor("rstd", [128, 1], F32))
            B["lnv"] = E1(nc.sbuf_tensor("lnv", [128, 4], F32))
            B["rtmp"] = E1(nc.sbuf_tensor("rtmp", [128, 4], F32))
            B["P"] = [E1(nc.psum_tensor(f"P{i}", [128, 1024], F32)) for i in range(4)]

            p1 = Prog()
            gen_phase1(p1, B, units)
            finals1 = [(s, p1.c(s)) for s in
                       ("spe", "sdv", "sac", "scw", "sbw", "sx0", "sx1", "scs",
                        "sco", "ssi", "swk0", "swk1", "swv0", "swv1",
                        "swq0", "swq1", "sbl0", "sbl1", "sz")]
            with nc.Block("ph1") as blk1:
                replay(p1, blk1, sems)

        with ExitStack() as st2:
            E2 = st2.enter_context
            B["ctxT"] = E2(nc.sbuf_tensor("ctxT", [128, 8 * DC * 128], BF16))
            B["ctxn8"] = E2(nc.sbuf_tensor("ctxn8", [128, 8 * D], BF16))
            B["wos"] = [E2(nc.sbuf_tensor(f"wos{i}", [128, 8192], BF16)) for i in range(2)]
            B["xst"] = E2(nc.sbuf_tensor("xst", [128, 2048], F32))
            B["rst"] = E2(nc.sbuf_tensor("rst", [128, 2048], F32))
            B["bn2"] = E2(nc.sbuf_tensor("bn2", [128, 768], F32))
            B["ag2"] = E2(nc.sbuf_tensor("ag2", [128, 2], F32))
            B["lnv2"] = E2(nc.sbuf_tensor("lnv2", [128, 4], F32))
            B["P2m"] = [E2(nc.psum_tensor(f"P2m{i}", [128, 1024], F32)) for i in range(2)]
            B["PT2"] = [E2(nc.psum_tensor(f"PT2{i}", [128, 1024], F32)) for i in range(2)]

            p2 = Prog(p1.cnt)
            for e in ENGS:
                p2.fence(e, finals1)
            gen_phase2(p2, B, units)
            finals2 = [(s, p2.c(s)) for s in
                       ("spe", "sdv", "sac", "sctx", "sxs", "s2r", "s2w0", "s2w1",
                        "sbw", "sbl0", "sbl1", "scc1")]
            with nc.Block("ph2") as blk2:
                replay(p2, blk2, sems)

        with ExitStack() as st3:
            E3 = st3.enter_context
            B["hT"] = E3(nc.sbuf_tensor("hT", [128, FT * 512], BF16))
            B["LN2T"] = E3(nc.sbuf_tensor("LN2T", [128, DC * 512], BF16))
            B["silb"] = E3(nc.sbuf_tensor("silb", [128, 512], BF16))
            B["P3p"] = [E3(nc.psum_tensor(f"P3p{i}", [128, 1024], F32)) for i in range(4)]

            p3 = Prog(p2.cnt)
            for e in ENGS:
                p3.fence(e, finals2)
            gen_phase3(p3, B, nc)
            with nc.Block("ph3") as blk3:
                replay(p3, blk3, sems)

    return nc


def make_units(B):
    units = []
    for src, dst in ((B["wgf_d"], B["scr_g"]), (B["wuf_d"], B["scr_u"])):
        for u in range(FT):
            for hf in range(2):
                units.append(("gu", src, dst, (u, hf)))
    for s in range(32):
        for w8 in range(8):
            units.append(("d", None, None, (s, w8)))
    return units


def gen_units(p, B, units, st, n):
    idb, slin, stg = B["idb"], B["slin"], B["stg"]
    for _ in range(n):
        if st["ui"] >= len(units):
            return
        kind, src, dst, u = units[st["ui"]]
        sl = st["ui"] % 2
        st["ui"] += 1
        if kind == "gu":
            u, hf = u
            rd = p.dma("pool", (lambda src=src, u=u, hf=hf, sl=sl: lambda e: e.dma_start(
                out=slin[sl][:, :],
                in_=src[u * 128:(u + 1) * 128, hf * 2048:(hf + 1) * 2048]))(),
                waits=[("spe", st["tp"][sl])], inc=f"sbl{sl}")
            nt = 16
        else:
            s, w8 = u
            rd = p.dma("pool", (lambda s=s, w8=w8, sl=sl: lambda e: e.dma_start(
                out=slin[sl][:, 0:1408],
                in_=B["wdf_d"][s * 128:(s + 1) * 128, w8 * 1408:(w8 + 1) * 1408]))(),
                waits=[("spe", st["tp"][sl])], inc=f"sbl{sl}")
            nt = 11
        ci = 0
        r = 0
        while ci < nt:
            nfill = min(4, nt - ci)
            pb = r % 2
            PT = st["PT"][pb]
            w = [("sdv", st["pbt"][pb])] + list(st.get("xw", []))
            if r == 0:
                w += [(f"sbl{sl}", rd), ("sbw", st["wr"][0])]
            for q in range(nfill):
                c = ci + q
                inc = "spe" if q == nfill - 1 else None
                p.pe((lambda sl=sl, c=c, q=q, PT=PT: lambda e: e.matmul(
                    PT[:, q * 128:(q + 1) * 128],
                    slin[sl][:, c * 128:(c + 1) * 128], idb[:, :],
                    start=True, stop=True))(),
                    waits=w if q == 0 else [], inc=inc)
            tv = p.c("spe")
            st["pbt"][pb] = p.ev("dve", (lambda sl=sl, ci=ci, PT=PT, nfill=nfill:
                                         lambda e: e.tensor_copy(
                stg[sl][:, ci * 128:(ci + nfill) * 128],
                PT[:, 0:nfill * 128]))(), waits=[("spe", tv)])
            ci += nfill
            r += 1
        st["xw"] = []
        st["tp"][sl] = p.c("spe")
        sev = p.c("sdv")
        if kind == "gu":
            c0 = (u * 32 + hf * 16) * 128
            st["wr"][0] = p.dma("act", (lambda dst=dst, c0=c0, sl=sl:
                                        lambda e: e.dma_start(
                out=dst[:, c0:c0 + 2048], in_=stg[sl][:, :]))(),
                waits=[("sdv", sev)], inc="sbw")
        else:
            base = (s // 8) * 88 + w8 * 11
            dsta = B["scr_d"].rearrange("p (r f) -> p r f", f=1024)
            st["wr"][0] = p.dma("act", (lambda dsta=dsta, base=base, s=s, sl=sl:
                                        lambda e: e.dma_start(
                out=dsta[:, base:base + 11, (s % 8) * 128:(s % 8) * 128 + 128],
                in_=stg[sl][:, 0:1408].rearrange("p (r f) -> p r f", r=11)))(),
                waits=[("sdv", sev)], inc="sbw")


def gen_phase1(p, B, units):
    x_d, wq_d, wk_d, wv_d = B["x_d"], B["wq_d"], B["wk_d"], B["wv_d"]
    cos_d, sin_d = B["cos_d"], B["sin_d"]
    consts, idb, rt, msk, zero8 = B["consts"], B["idb"], B["rt"], B["msk"], B["zero8"]
    K, Vh, LNXT, QT = B["K"], B["Vh"], B["LNXT"], B["QT"]
    xbuf, lnb, wqs, wks, wvs = B["xbuf"], B["lnb"], B["wqs"], B["wks"], B["wvs"]
    cosg, sing, praw, t1b, t2b = B["cosg"], B["sing"], B["praw"], B["t1b"], B["t2b"]
    Pt, cstg = B["Pt"], B["cstg"]
    bnb, agb, rstd, lnv4, rtmp = B["bnb"], B["agb"], B["rstd"], B["lnv"], B["rtmp"]
    P = B["P"]
    ctx_shr = B["ctx_shr"]

    ustate = {"ui": 0, "tp": [0, 0], "pbt": [0, 0], "wr": [0, 0], "xw": [],
              "PT": [P[3][:, 0:512], P[3][:, 512:1024]]}

    for j, v in enumerate([1.0, 0.0, EXP_BIAS, EXP_SCALE, EPS, -0.5]):
        p.ev("dve", (lambda j=j, v=v: lambda e: e.memset(consts[:, j:j + 1], v))())
    p.ev("dve", lambda e: e.memset(zero8[:, :], 0.0))
    v_init = p.ev("dve", lambda e: e.memset(Vh[:, :], 1.0))
    p.dma("sp", lambda e: e.dma_start(out=msk[:, :], in_=B["msk_d"]), inc="scs")
    p.dma("sp", lambda e: e.dma_start(out=idb[:, :], in_=B["idb_d"]), inc="scs")
    c_init = p.dma("sp", lambda e: e.dma_start(out=rt[:, :], in_=B["rt_d"]), inc="scs")
    p.dma("pool", lambda e: e.dma_start(out=B["b1i"], in_=zero8[:, :]),
          waits=[("sdv", v_init)], inc="sz")
    p.ev("act", lambda e: e.preload_act_table(AF.Exp))

    LNXT_r = LNXT[:, :].rearrange("p (c f) -> p c f", c=DC)
    Vh_r = Vh[:, :].rearrange("p (b f) -> p b f", f=132)

    tp_done = [0] * (NT + 2)
    ap_done = [0] * (NT + 2)
    x_val = [0] * (NT + 2)
    evac_val = 0
    scw_val = 0
    rope_done = 0
    q_done = 0

    def emit_x(t):
        if t >= NT:
            return
        w = [("sdv", ap_done[t - 2])] if t >= 2 else []
        x_val[t] = p.dma("sp", (lambda t=t: lambda e: e.dma_start(
            out=xbuf[t % 2][:, :], in_=x_d[t * 128:(t + 1) * 128, :]))(), waits=w,
            inc=f"sx{t % 2}")

    emit_x(0)
    emit_x(1)

    for gi in range(NG):
        t0 = 4 * gi
        kmax = t0 + 4

        co_val = p.dma("sp", (lambda gi=gi: lambda e: e.dma_start(
            out=cosg[:, :], in_=cos_d[:, gi * 512:(gi + 1) * 512]))(),
            waits=[("sdv", rope_done)], inc="sco")
        cs_val = p.dma("sp", (lambda gi=gi: lambda e: e.dma_start(
            out=sing[:, :], in_=sin_d[:, gi * 512:(gi + 1) * 512]))(), inc="ssi")

        for j in range(4):
            t = t0 + j
            for ch in range(8):
                w = [(f"sx{t % 2}", x_val[t])] if ch == 0 else []
                p.ev("dve", (lambda t=t, ch=ch: lambda e: e.bn_stats(
                    bnb[:, ch * 6:(ch + 1) * 6],
                    xbuf[t % 2][:, ch * 512:(ch + 1) * 512]))(), waits=w)
            ag_val = p.ev("dve", lambda e: e.bn_aggr(agb[:, :], bnb[:, :]),
                          waits=[("sdv", p.c("sdv"))])
            rs_val = emit_rsqrt(p, agb[:, 1:2], rstd[:, 0:1], lnv4)
            ap_done[t] = p.ev("dve", (lambda t=t: lambda e: e.tensor_scalar(
                lnb[:, :], xbuf[t % 2][:, :], agb[:, 0:1], rstd[:, 0:1],
                op0=ALU.subtract, op1=ALU.mult))(),
                waits=[("sdv", rs_val), ("spe", tp_done[t - 1] if t else 0)])
            for r in range(8):
                hb = r % 2
                w = [("sdv", ustate["pbt"][hb])]
                if r == 0:
                    w += [("sdv", ap_done[t]), ("scs", c_init)]
                for q in range(4):
                    c = 4 * r + q
                    inc = "spe" if q == 3 else None
                    p.pe((lambda t=t, c=c, hb=hb, q=q: lambda e: e.matmul(
                        P[3][:, hb * 512 + q * 128:hb * 512 + (q + 1) * 128],
                        lnb[:, c * 128:(c + 1) * 128], idb[:, :],
                        start=True, stop=True))(),
                        waits=w if q == 0 else [], inc=inc)
                tv = p.c("spe")
                src = P[3][:, hb * 512:(hb + 1) * 512].rearrange("p (c f) -> p c f", c=4)
                ustate["pbt"][hb] = p.ev("dve", (lambda r=r, j=j, src=src: lambda e: e.tensor_copy(
                    LNXT_r[:, 4 * r:4 * r + 4, j * 128:(j + 1) * 128], src))(),
                    waits=[("spe", tv)])
            tp_done[t] = p.c("spe")
            emit_x(t + 2)
        lnxt_val = p.c("sdv")

        def kv_dma(c, w):
            p.dma("pool", (lambda c=c: lambda e: e.dma_start(
                out=wks[c % 2][:, :], in_=wk_d[c * 128:(c + 1) * 128, :]))(),
                waits=w, inc=f"swk{c % 2}")
            p.dma("pool", (lambda c=c: lambda e: e.dma_start(
                out=wvs[c % 2][:, :], in_=wv_d[c * 128:(c + 1) * 128, :]))(),
                inc=f"swv{c % 2}")
            return [(f"swk{c % 2}", p.c(f"swk{c % 2}")), (f"swv{c % 2}", p.c(f"swv{c % 2}"))]

        kv_rd = [None] * DC
        kv_rd[0] = kv_dma(0, [("spe", p.c("spe"))])
        kv_rd[1] = kv_dma(1, [])
        for c in range(DC):
            w = list(kv_rd[c])
            if c == 0:
                w += [("sdv", lnxt_val), ("sdv", evac_val), ("sdv", v_init),
                      ("sdv", ustate["pbt"][0]), ("sdv", ustate["pbt"][1])]
            for kv in range(4):
                Pk, sl = P[kv // 2], (kv % 2) * 512
                p.pe((lambda c=c, kv=kv, Pk=Pk, sl=sl: lambda e: e.matmul(
                    Pk[:, sl:sl + 512],
                    wks[c % 2][:, kv * 128:(kv + 1) * 128],
                    LNXT[:, c * 512:(c + 1) * 512],
                    start=(c == 0), stop=(c == DC - 1)))(),
                    waits=w if kv == 0 else [])
            for j in range(4):
                Pv, sl = P[2 + j // 2], (j % 2) * 512
                inc = "spe" if j == 3 else None
                p.pe((lambda c=c, j=j, Pv=Pv, sl=sl: lambda e: e.matmul(
                    Pv[:, sl:sl + 512],
                    LNXT[:, c * 512 + j * 128:c * 512 + (j + 1) * 128],
                    wvs[c % 2][:, :],
                    start=(c == 0), stop=(c == DC - 1)))(), inc=inc)
            if c + 2 < DC:
                kv_rd[c + 2] = kv_dma(c + 2, [("spe", p.c("spe"))])
        kv_done = p.c("spe")

        vev = []
        for j in range(4):
            Pv, sl = P[2 + j // 2], (j % 2) * 512
            src = Pv[:, sl:sl + 512].rearrange("p (b f) -> p b f", f=128)
            dst = Vh_r[:, (t0 + j) * 4:(t0 + j) * 4 + 4, 0:128]
            vev.append(p.ev("dve", (lambda src=src, dst=dst: lambda e: e.tensor_copy(dst, src))(),
                            waits=[("spe", kv_done)]))

        kq_last = [0, 0]
        for j in range(4):
            Pk, sl = P[j // 2], (j % 2) * 512
            w = [("spe", kv_done)]
            if j >= 2:
                w.append(("sdv", kq_last[j % 2]))
            cp = p.ev("dve", (lambda Pk=Pk, sl=sl, j=j: lambda e: e.tensor_copy(
                praw[j % 2][:, :], Pk[:, sl:sl + 512]))(), waits=w)
            rw = [("sdv", cp), ("sdv", vev[j])]
            if j >= 2:
                rw.append(("sdv", ustate["pbt"][j % 2]))
            rot = p.pe((lambda j=j: lambda e: e.matmul(
                P[2 + j // 2][:, (j % 2) * 512:(j % 2) * 512 + 512],
                rt[:, :], praw[j % 2][:, :], start=True, stop=True))(),
                waits=rw, inc="spe")
            p.ev("dve", (lambda j=j: lambda e: e.tensor_mul(
                t1b[0][:, :], praw[j % 2][:, :], cosg[:, :]))(),
                waits=[("sco", co_val), ("ssi", cs_val), ("sdv", kq_last[(j + 1) % 2])])
            p.ev("dve", (lambda j=j: lambda e: e.tensor_mul(
                t2b[0][:, :], P[2 + j // 2][:, (j % 2) * 512:(j % 2) * 512 + 512],
                sing[:, :]))(), waits=[("spe", rot)])
            kq_last[j % 2] = p.ev("dve", (lambda j=j, t0=t0: lambda e: e.tensor_add(
                K[:, j * T + t0 * 128:j * T + t0 * 128 + 512],
                t1b[0][:, :], t2b[0][:, :]))(),
                waits=[("sdv", p.c("sdv"))])
        k_done = p.c("sdv")

        for half in range(2):
            def q_dma(bi, w, half=half):
                srcsl = wq_d[bi * 256:(bi + 1) * 256, half * 1024:(half + 1) * 1024]
                srca = srcsl.rearrange("(c p) f -> p c f", p=128)
                dsta = wqs[bi % 2][:, :].rearrange("p (c f) -> p c f", c=2)
                p.dma("pool", (lambda srca=srca, dsta=dsta: lambda e: e.dma_start(
                    out=dsta, in_=srca))(), waits=w, inc=f"swq{bi % 2}")
                return (f"swq{bi % 2}", p.c(f"swq{bi % 2}"))

            q_rd = [None] * 16
            q_rd[0] = q_dma(0, [("spe", p.c("spe"))])
            q_rd[1] = q_dma(1, [])
            for c in range(DC):
                bi = c // 2
                w = [q_rd[bi]]
                if c == 0:
                    w.append(("sdv", k_done if half == 0 else q_done))
                for k8 in range(8):
                    Pq, sl = P[k8 // 2], (k8 % 2) * 512
                    inc = "spe" if (c % 2 == 1 and k8 == 7) else None
                    p.pe((lambda c=c, k8=k8, half=half, Pq=Pq, sl=sl: lambda e: e.matmul(
                        Pq[:, sl:sl + 512],
                        wqs[(c // 2) % 2][:, (c % 2) * 1024 + k8 * 128:(c % 2) * 1024 + (k8 + 1) * 128],
                        LNXT[:, c * 512:(c + 1) * 512],
                        start=(c == 0), stop=(c == DC - 1)))(),
                        waits=w if k8 == 0 else [], inc=inc)
                if c % 2 == 1 and c // 2 + 2 < 16:
                    q_rd[c // 2 + 2] = q_dma(c // 2 + 2, [("spe", p.c("spe"))])
            qproj = p.c("spe")
            rot_last = [0, 0]
            for k8 in range(8):
                qh = half * 8 + k8
                Pq, sl = P[k8 // 2], (k8 % 2) * 512
                w = [("spe", qproj)]
                if k8 >= 2:
                    w.append(("sdv", rot_last[k8 % 2]))
                cp = p.ev("dve", (lambda Pq=Pq, sl=sl, k8=k8: lambda e: e.tensor_copy(
                    praw[k8 % 2][:, :], Pq[:, sl:sl + 512]))(), waits=w)
                rw = [("sdv", cp)]
                if k8 >= 6:
                    rw.append(("sdv", ustate["pbt"][k8 % 2]))
                rot = p.pe((lambda Pq=Pq, sl=sl, k8=k8: lambda e: e.matmul(
                    Pq[:, sl:sl + 512], rt[:, :], praw[k8 % 2][:, :],
                    start=True, stop=True))(), waits=rw, inc="spe")
                p.ev("dve", (lambda k8=k8: lambda e: e.tensor_mul(
                    t1b[0][:, :], praw[k8 % 2][:, :], cosg[:, :]))(),
                    waits=[("sdv", rot_last[(k8 + 1) % 2])])
                p.ev("dve", (lambda Pq=Pq, sl=sl, k8=k8: lambda e: e.tensor_mul(
                    t2b[0][:, :], Pq[:, sl:sl + 512], sing[:, :]))(),
                    waits=[("spe", rot)])
                rot_last[k8 % 2] = p.ev("dve", (lambda qh=qh, k8=k8: lambda e: e.tensor_add(
                    QT[:, qh * 512:(qh + 1) * 512], t1b[0][:, :], t2b[0][:, :]))(),
                    waits=[("sdv", p.c("sdv"))])
            q_done = p.c("sdv")
        rope_done = q_done

        s_read = [0, 0]
        pv_val = [0, 0]
        for h in range(NQH):
            kvh = h // 4
            npair = kmax // 2
            pvfin = 0
            for pp in range(npair):
                par = pp % 2
                kc0 = 2 * pp
                w = [("sac", s_read[par])]
                if pp == 0 and h == 0:
                    w.append(("sdv", q_done))
                for ii in range(2):
                    kc = kc0 + ii
                    inc = "spe" if ii == 1 else None
                    p.pe((lambda kvh=kvh, kc=kc, h=h, par=par, ii=ii: lambda e: e.matmul(
                        P[par][:, ii * 512:(ii + 1) * 512],
                        K[:, kvh * T + kc * 128:kvh * T + (kc + 1) * 128],
                        QT[:, h * 512:(h + 1) * 512], start=True, stop=True))(),
                        waits=w if ii == 0 else [], inc=inc)
                sval = p.c("spe")
                ev = p.ev("act", (lambda par=par: lambda e: e.activation(
                    Pt[par][:, :], P[par][:, :], AF.Exp,
                    bias=EXP_BIAS, scale=EXP_SCALE))(),
                    waits=[("spe", sval), ("spe", pv_val[par])])
                s_read[par] = ev
                pv_dep = ("sac", ev)
                if kc0 >= t0:
                    for ii in range(2):
                        mj = kc0 + ii - t0
                        mv = p.ev("dve", (lambda par=par, ii=ii, mj=mj: lambda e: e.tensor_mul(
                            Pt[par][:, ii * 512:(ii + 1) * 512],
                            Pt[par][:, ii * 512:(ii + 1) * 512],
                            msk[:, 384 - 128 * mj:896 - 128 * mj]))(),
                            waits=[("sac", ev)])
                    pv_dep = ("sdv", mv)
                w = [pv_dep]
                if pp == 0:
                    w.append(("sdv", evac_val))
                for ii in range(2):
                    kc = kc0 + ii
                    for jj in range(4):
                        o = CTX_OFF[jj]
                        inc = "spe" if (ii == 1 and jj == 3) else None
                        st_f = (kc == 0 and jj % 2 == 0)
                        sp_f = (kc == kmax - 1 and jj % 2 == 1)
                        p.pe((lambda par=par, ii=ii, jj=jj, o=o, kc=kc, kvh=kvh,
                                     st_f=st_f, sp_f=sp_f:
                              lambda e: e.matmul(
                                  P[2][:, o:o + 129],
                                  Pt[par][:, ii * 512 + jj * 128:ii * 512 + (jj + 1) * 128],
                                  Vh[:, (kc * 4 + kvh) * 132:(kc * 4 + kvh) * 132 + 129],
                                  start=st_f, stop=sp_f))(),
                            waits=w if (ii == 0 and jj == 0) else [], inc=inc)
                pv_val[par] = p.c("spe")
                pvfin = pv_val[par]
            for jj in range(4):
                o = CTX_OFF[jj]
                w = [("spe", pvfin)]
                if h == 0:
                    w.append(("scw", scw_val))
                rv = p.ev("dve", (lambda o=o, jj=jj: lambda e: e.reciprocal(
                    rtmp[:, jj:jj + 1], P[2][:, o + 128:o + 129]))(), waits=w)
                evac_val = p.ev("dve", (lambda o=o, jj=jj, h=h: lambda e: e.tensor_scalar(
                    cstg[:, jj * 2048 + h * 128:jj * 2048 + (h + 1) * 128],
                    P[2][:, o:o + 128],
                    rtmp[:, jj:jj + 1], None, op0=ALU.mult))(),
                    waits=[("sdv", rv)])
            if h == 7:
                ustate["xw"] = [("sdv", q_done)]
                gen_units(p, B, units, ustate, 14)

        def ctx_write(e, tt):
            k = (id(e), "ctx")
            if k not in _ECACHE:
                pid = e.snap(e.partition_id() % 2, min_val=0, max_val=1)
                base = e.snap(pid * (T * NQH * DH), min_val=0, max_val=T * NQH * DH)
                reg = e.alloc_register("ctxoff")
                _ECACHE[k] = (base, reg)
            base, reg = _ECACHE[k]
            e.reg_alu(reg, base, tt * 128 * 2048, ALU.add)
            out_ap = bass.AP(ctx_shr.tensor, reg,
                             [[2048, 128], [128 * 2048, 4], [1, 2048]])
            return e.dma_start(out=out_ap,
                               in_=cstg[:, :].rearrange("p (t f) -> p t f", t=4))
        scw_val = p.dma("sp", (lambda tt=t0: lambda e: ctx_write(e, tt))(),
                        waits=[("sdv", evac_val)], inc="scw")
        ustate["xw"] = [("sdv", q_done)]
        gen_units(p, B, units, ustate, 14)

    p.ev("pool", lambda e: e.collective_compute(
        "AllReduce", ALU.add, replica_groups=PAIRS,
        ins=[B["b1i"].opt()], outs=[B["b1o"].opt()]),
        waits=[("scw", p.c("scw")), ("sdv", p.c("sdv")), ("sz", p.c("sz"))],
        inc="scc1")
    B["_ustate"] = ustate


def gen_phase2(p, B, units):
    xo_d, wo_d, out_d, ctx_shr = B["xo_d"], B["wo_d"], B["out_d"], B["ctx_shr"]
    idb, stats2 = B["idb"], B["stats2"]
    ctxT, ctxn8, wos, xst, rst = B["ctxT"], B["ctxn8"], B["wos"], B["xst"], B["rst"]
    bn2, ag2, lnv2 = B["bn2"], B["ag2"], B["lnv2"]
    P2m, PT2 = B["P2m"], B["PT2"]
    ustate = B["_ustate"]
    ustate["PT"] = [PT2[1][:, 0:512], PT2[1][:, 512:1024]]
    ustate["xw"] = []

    for e in ENGS:
        p.fence(e, [("scc1", 1)])

    xo_r = xo_d.rearrange("(r p) f -> p r f", p=128)
    osrc = out_d.rearrange("(r p) f -> p r f", p=128)

    def ctx_read(e, sl2, q8):
        k = (id(e), "ctxr")
        if k not in _ECACHE:
            pid = e.snap(e.partition_id() % 2, min_val=0, max_val=1)
            base = e.snap(pid * (2048 * 2048), min_val=0, max_val=2048 * 2048)
            reg = e.alloc_register("ctxroff")
            _ECACHE[k] = (base, reg)
        base, reg = _ECACHE[k]
        e.reg_alu(reg, base, (sl2 * T + q8 * 1024) * 2048, ALU.add)
        in_ap = bass.AP(ctx_shr.tensor, reg,
                        [[2048, 128], [128 * 2048, 8], [1, 2048]])
        dst = ctxn8[:, :].rearrange("p (r s f) -> p r s f", r=8, s=2)[:, :, sl2, :]
        return e.dma_start(out=dst, in_=in_ap)

    tpq_done = [0] * 18
    ctr2 = [0, 0]
    sx_mm = [0] * 40
    ev_slot = [0, 0, 0, 0]
    x_rd = 0
    r_wr = 0
    bn_done = 0
    for q8 in range(2):
        crd = 0
        for sl2 in range(2):
            crd = p.dma("sp", (lambda sl2=sl2, q8=q8: lambda e: ctx_read(e, sl2, q8))(),
                        waits=[("spe", tpq_done[7])] if (q8 and sl2 == 0) else [],
                        inc="sctx")
        for qt in range(8):
            qtg = q8 * 8 + qt
            for r in range(8):
                pb = r % 2
                w2 = [("sdv", ctr2[pb])]
                if r == 0 and qt == 0:
                    w2.append(("sctx", crd))
                for q in range(4):
                    c = 4 * r + q
                    inc = "spe" if q == 3 else None
                    p.pe((lambda qt=qt, c=c, pb=pb, q=q: lambda e: e.matmul(
                        PT2[0][:, pb * 512 + q * 128:pb * 512 + (q + 1) * 128],
                        ctxn8[:, qt * 4096 + c * 128:qt * 4096 + (c + 1) * 128],
                        idb[:, :], start=True, stop=True))(),
                        waits=w2 if q == 0 else [], inc=inc)
                tv = p.c("spe")
                ctr2[pb] = p.ev("dve", (lambda qt=qt, r=r, pb=pb: lambda e: e.tensor_copy(
                    ctxT[:, qt * 4096 + r * 512:qt * 4096 + (r + 1) * 512],
                    PT2[0][:, pb * 512:(pb + 1) * 512]))(), waits=[("spe", tv)])
            tpq_done[qtg] = p.c("spe")
        ctxT_all = p.c("sdv")

        for sx in range(16):
            sxg = q8 * 16 + sx
            wsl = sxg % 2
            for cb in range(4):
                srcsl = wo_d[cb * 1024:(cb + 1) * 1024, sx * 256:(sx + 1) * 256]
                srca = srcsl.rearrange("(c p) f -> p c f", p=128)
                dsta = wos[wsl][:, :].rearrange("p (c f) -> p c f", c=32)
                p.dma("pool", (lambda srca=srca, dsta=dsta, cb=cb: lambda e: e.dma_start(
                    out=dsta[:, cb * 8:(cb + 1) * 8, :], in_=srca))(),
                    waits=[("spe", sx_mm[sxg])] if cb == 0 else [], inc=f"s2w{wsl}")
            wo_val = p.c(f"s2w{wsl}")
            x_rd = p.dma("sp", (lambda q8=q8, sx=sx: lambda e: e.dma_start(
                out=xst[:, :].rearrange("p (r f) -> p r f", r=8),
                in_=xo_r[:, q8 * 8:q8 * 8 + 8, sx * 256:(sx + 1) * 256]))(),
                waits=[("sdv", bn_done)], inc="sxs")
            for qt in range(8):
                psl, off = qt % 2, ((qt // 2) % 2) * 512
                w = [(f"s2w{wsl}", wo_val), ("sdv", ctxT_all), ("sdv", ev_slot[qt % 4])]
                for c in range(DC):
                    inc = "spe" if c == DC - 1 else None
                    p.pe((lambda qt=qt, c=c, psl=psl, off=off, wsl=wsl: lambda e: e.matmul(
                        P2m[psl][:, off:off + 256],
                        ctxT[:, qt * 4096 + c * 128:qt * 4096 + (c + 1) * 128],
                        wos[wsl][:, c * 256:(c + 1) * 256],
                        start=(c == 0), stop=(c == DC - 1)))(),
                        waits=w if c == 0 else [], inc=inc)
                mmv = p.c("spe")
                ev_slot[qt % 4] = p.ev("dve", (lambda qt=qt, psl=psl, off=off:
                                               lambda e: e.tensor_add(
                    rst[:, qt * 256:(qt + 1) * 256], P2m[psl][:, off:off + 256],
                    xst[:, qt * 256:(qt + 1) * 256]))(),
                    waits=[("spe", mmv), ("sxs", x_rd), ("s2r", r_wr)])
                p.ev("dve", (lambda qt=qt, sx=sx: lambda e: e.bn_stats(
                    bn2[:, (qt * 16 + sx) * 6:(qt * 16 + sx + 1) * 6],
                    rst[:, qt * 256:(qt + 1) * 256]))())
            bn_done = p.c("sdv")
            sx_mm[sxg + 2] = p.c("spe")
            r_wr = p.dma("sp", (lambda q8=q8, sx=sx: lambda e: e.dma_start(
                out=osrc[:, q8 * 8:q8 * 8 + 8, sx * 256:(sx + 1) * 256],
                in_=rst[:, :].rearrange("p (r f) -> p r f", r=8)))(),
                waits=[("sdv", bn_done)], inc="s2r")
            gen_units(p, B, units, ustate, 5)
        for qt in range(8):
            qtg = q8 * 8 + qt
            p.ev("dve", (lambda qt=qt: lambda e: e.bn_aggr(
                ag2[:, :], bn2[:, qt * 96:(qt + 1) * 96]))(),
                waits=[("sdv", bn_done)])
            p.ev("dve", (lambda qtg=qtg: lambda e: e.tensor_copy(
                stats2[:, qtg * 2:qtg * 2 + 1], ag2[:, 0:1]))())
            emit_rsqrt(p, ag2[:, 1:2], stats2[:, qtg * 2 + 1:qtg * 2 + 2], lnv2)

    gen_units(p, B, units, ustate, 10 ** 6)


def gen_phase3(p, B, nc):
    out_d = B["out_d"]
    idb, stats2 = B["idb"], B["stats2"]
    hT, LN2T = B["hT"], B["LN2T"]
    silb = B["silb"]
    P3p = B["P3p"]

    p.ev("act", lambda e: e.preload_act_table(AF.Silu))

    LN2T_r = LN2T[:, :].rearrange("p (c f) -> p c f", c=DC)
    out_r = out_d.rearrange("(r p) f -> p r f", p=128)
    scr_g_r = B["scr_g"]
    scr_u_r = B["scr_u"]
    scr_d_r = B["scr_d"]

    r_rd = [0, 0]
    tp_last = 0
    ev_round = [0, 0]
    ap_prev = 0
    wg_used = [0, 0]
    h_last = [0, 0]
    wd_used = [0, 0]
    y_wr = 0
    down_ev = 0
    for fg in range(4):
      with ExitStack() as sec1:
        rbuf = [sec1.enter_context(nc.sbuf_tensor(f"rbuf{fg}{i}", [128, D], F32))
                for i in range(2)]
        ln2b = sec1.enter_context(nc.sbuf_tensor(f"ln2b{fg}", [128, D], BF16))
        for j in range(4):
            qtg = fg * 4 + j
            w = [("s2r", p.c("s2r"))]
            if j >= 2 or fg > 0:
                w.append(("sdv", ap_prev))
            if j == 0 and fg > 0:
                w += [("sdv", down_ev), ("syw", y_wr)]
            r_rd[j % 2] = p.dma("sp", (lambda qtg=qtg, j=j: lambda e: e.dma_start(
                out=rbuf[j % 2][:, :], in_=out_d[qtg * 128:(qtg + 1) * 128, :]))(),
                waits=w, inc=f"srb{j % 2}")
            apv = p.ev("dve", (lambda qtg=qtg, j=j: lambda e: e.tensor_scalar(
                ln2b[:, :], rbuf[j % 2][:, :],
                stats2[:, qtg * 2:qtg * 2 + 1], stats2[:, qtg * 2 + 1:qtg * 2 + 2],
                op0=ALU.subtract, op1=ALU.mult))(),
                waits=[(f"srb{j % 2}", r_rd[j % 2]), ("spe", tp_last)])
            ap_prev = apv
            for r in range(8):
                hb = r % 2
                w2 = [("sdv", ev_round[hb])]
                if r == 0:
                    w2.append(("sdv", apv))
                if r == 0 and j == 0 and fg > 0:
                    w2.append(("sdv", down_ev))
                for q in range(4):
                    c = 4 * r + q
                    inc = "spe" if q == 3 else None
                    p.pe((lambda j=j, c=c, hb=hb, q=q: lambda e: e.matmul(
                        P3p[2][:, hb * 512 + q * 128:hb * 512 + (q + 1) * 128],
                        ln2b[:, c * 128:(c + 1) * 128], idb[:, :],
                        start=True, stop=True))(),
                        waits=w2 if q == 0 else [], inc=inc)
                tv = p.c("spe")
                src = P3p[2][:, hb * 512:(hb + 1) * 512].rearrange("p (c f) -> p c f", c=4)
                ev_round[hb] = p.ev("dve", (lambda r=r, j=j, src=src: lambda e: e.tensor_copy(
                    LN2T_r[:, 4 * r:4 * r + 4, j * 128:(j + 1) * 128], src))(),
                    waits=[("spe", tv)])
            tp_last = p.c("spe")
        ln2_done = p.c("sdv")

      with ExitStack() as sec2:
        wgs = [sec2.enter_context(nc.sbuf_tensor(f"wgs{fg}{i}", [128, 4096], BF16))
               for i in range(2)]
        wus = [sec2.enter_context(nc.sbuf_tensor(f"wus{fg}{i}", [128, 4096], BF16))
               for i in range(2)]
        for ft in range(FT):
            sl = ft % 2
            w0 = [("spe", wg_used[sl])]
            if ft < 2:
                w0.append(("spe", tp_last))
            p.dma("sp", (lambda ft=ft, sl=sl: lambda e: e.dma_start(
                out=wgs[sl][:, :],
                in_=scr_g_r[:, ft * 4096:(ft + 1) * 4096]))(),
                waits=w0, inc=f"s3g{sl}")
            wrd = p.dma("sp", (lambda ft=ft, sl=sl: lambda e: e.dma_start(
                out=wus[sl][:, :],
                in_=scr_u_r[:, ft * 4096:(ft + 1) * 4096]))(), inc=f"s3g{sl}")
            Pp = P3p[sl]
            w = [(f"s3g{sl}", wrd), ("sdv", h_last[sl])]
            if ft == 0:
                w.append(("sdv", ln2_done))
            for c in range(DC):
                inc = "spe" if c == DC - 1 else None
                p.pe((lambda c=c, sl=sl, Pp=Pp: lambda e: e.matmul(
                    Pp[:, 0:512], wgs[sl][:, c * 128:(c + 1) * 128],
                    LN2T[:, c * 512:(c + 1) * 512],
                    start=(c == 0), stop=(c == DC - 1)))(),
                    waits=w if c == 0 else [])
                p.pe((lambda c=c, sl=sl, Pp=Pp: lambda e: e.matmul(
                    Pp[:, 512:1024], wus[sl][:, c * 128:(c + 1) * 128],
                    LN2T[:, c * 512:(c + 1) * 512],
                    start=(c == 0), stop=(c == DC - 1)))(), inc=inc)
            mmv = p.c("spe")
            wg_used[sl] = mmv
            sv = p.ev("act", (lambda Pp=Pp: lambda e: e.activation(
                silb[:, :], Pp[:, 0:512], AF.Silu, bias=0.0, scale=1.0))(),
                waits=[("spe", mmv), ("sdv", h_last[1 - sl])])
            h_last[sl] = p.ev("dve", (lambda ft=ft, Pp=Pp: lambda e: e.tensor_mul(
                hT[:, ft * 512:(ft + 1) * 512], silb[:, :], Pp[:, 512:1024]))(),
                waits=[("sac", sv)])
        h_all = p.c("sdv")
        gate_fin = p.c("spe")

      with ExitStack() as sec3:
        wds = [sec3.enter_context(nc.sbuf_tensor(f"wds{fg}{i}", [128, 4096], BF16))
               for i in range(2)]
        rstg = sec3.enter_context(nc.sbuf_tensor(f"rstg{fg}", [128, 4096], F32))
        ystg = sec3.enter_context(nc.sbuf_tensor(f"ystg{fg}", [128, 4096], F32))
        for dq in range(4):
            w0 = [("syw", y_wr)]
            if dq == 0:
                w0 += [("spe", gate_fin), ("sdv", h_all)]
            rs_rd = p.dma("sp", (lambda fg=fg, dq=dq: lambda e: e.dma_start(
                out=rstg[:, :].rearrange("p (r f) -> p r f", r=4),
                in_=out_r[:, fg * 4:fg * 4 + 4, dq * 1024:(dq + 1) * 1024]))(),
                waits=w0, inc="srs")
            for k in range(22):
                sl = k % 2
                w1 = [("spe", wd_used[sl])]
                if dq == 0 and k < 2:
                    w1 += [("spe", gate_fin), ("sdv", h_all)]
                wd_rd = p.dma("act", (lambda dq=dq, k=k, sl=sl: lambda e: e.dma_start(
                    out=wds[sl][:, :],
                    in_=scr_d_r[:, (dq * 88 + 4 * k) * 1024:(dq * 88 + 4 * k + 4) * 1024]))(),
                    waits=w1, inc=f"s3w{sl}")
                w = [(f"s3w{sl}", wd_rd), ("sdv", h_all)]
                if k == 0:
                    w.append(("sdv", down_ev))
                for i in range(4):
                    fc = k * 4 + i
                    for qt in range(4):
                        for nn in range(2):
                            inc = "spe" if (i == 3 and qt == 3 and nn == 1) else None
                            p.pe((lambda fc=fc, qt=qt, nn=nn, i=i, sl=sl: lambda e: e.matmul(
                                P3p[qt][:, nn * 512:(nn + 1) * 512],
                                hT[:, fc * 512 + qt * 128:fc * 512 + (qt + 1) * 128],
                                wds[sl][:, i * 1024 + nn * 512:i * 1024 + (nn + 1) * 512],
                                start=(fc == 0), stop=(fc == FT - 1)))(),
                                waits=w if (i == 0 and qt == 0 and nn == 0) else [], inc=inc)
                wd_used[sl] = p.c("spe")
            dnv = p.c("spe")
            for qt in range(4):
                down_ev = p.ev("dve", (lambda qt=qt: lambda e: e.tensor_add(
                    ystg[:, qt * 1024:(qt + 1) * 1024], P3p[qt][:, :],
                    rstg[:, qt * 1024:(qt + 1) * 1024]))(),
                    waits=[("spe", dnv), ("srs", rs_rd)])
            y_wr = p.dma("sp", (lambda fg=fg, dq=dq: lambda e: e.dma_start(
                out=out_r[:, fg * 4:fg * 4 + 4, dq * 1024:(dq + 1) * 1024],
                in_=ystg[:, :].rearrange("p (r f) -> p r f", r=4)))(),
                waits=[("sdv", down_ev)], inc="syw")

    p.fence("sp", [("syw", p.c("syw"))])


def _rope_tables():
    idx = np.arange(0, DH, 2, dtype=np.float64) / DH
    freq = (1.0 / (2.0 * math.pi)) * (500000.0 ** (-idx))
    factor, lo, hi, L0 = 32.0, 1.0, 4.0, 8192.0
    freq_low, freq_high = lo / L0, hi / L0
    freq_scaled = np.where(freq < freq_low, freq / factor, freq)
    smooth = np.clip((L0 * freq - lo) / (hi - lo), 0.0, 1.0)
    freq_smooth = (1.0 - smooth) * (freq / factor) + smooth * freq
    is_mid = (freq >= freq_low) & (freq <= freq_high)
    fr = np.where(is_mid, freq_smooth, freq_scaled)
    t = np.arange(T, dtype=np.float64)
    phase = 2.0 * math.pi * t[:, None] * fr[None, :]
    emb = np.concatenate([phase, phase], axis=1)
    cos = np.ascontiguousarray(np.cos(emb).T).astype(BF)
    sin = np.ascontiguousarray(np.sin(emb).T).astype(BF)
    return cos, sin


def _masks():
    k = np.arange(128)[:, None]
    i = np.arange(896)[None, :]
    return (k <= (i - 384)).astype(BF)


def _rotmat():
    rt = np.zeros((128, 128), dtype=BF)
    for i in range(64):
        rt[i, i + 64] = 1.0
        rt[i + 64, i] = -1.0
    return rt


_NC_CACHE = []


def kernel(L0_attn_resid, L0_attn_Wq, L0_attn_Wk, L0_attn_Wv, L0_attn_Wo,
           L0_ffn_W_gate, L0_ffn_W_up, L0_ffn_W_down):
    x = np.ascontiguousarray(np.asarray(L0_attn_resid, dtype=np.float32))
    wq = np.ascontiguousarray(np.asarray(L0_attn_Wq, dtype=np.float32))
    wk = np.ascontiguousarray(np.asarray(L0_attn_Wk, dtype=np.float32))
    wv = np.ascontiguousarray(np.asarray(L0_attn_Wv, dtype=np.float32))
    wo = np.ascontiguousarray(np.asarray(L0_attn_Wo, dtype=np.float32))
    wg = np.asarray(L0_ffn_W_gate, dtype=np.float32)
    wu = np.asarray(L0_ffn_W_up, dtype=np.float32)
    wd = np.asarray(L0_ffn_W_down, dtype=np.float32)

    wg_p = np.zeros((F_PAD, D), np.float32); wg_p[:F_REAL] = wg
    wu_p = np.zeros((F_PAD, D), np.float32); wu_p[:F_REAL] = wu
    wd_p = np.zeros((D, F_PAD), np.float32); wd_p[:, :F_REAL] = wd
    wq_h = [np.ascontiguousarray(wq[:, i * 2048:(i + 1) * 2048]) for i in range(2)]
    wk_h = [np.ascontiguousarray(wk[:, i * 512:(i + 1) * 512]) for i in range(2)]
    wv_h = [np.ascontiguousarray(wv[:, i * 512:(i + 1) * 512]) for i in range(2)]

    cos, sin = _rope_tables()
    msks = _masks()
    rtm = _rotmat()
    idb = np.eye(128, dtype=BF)

    if not _NC_CACHE:
        _NC_CACHE.append(build_nc())
    nc = _NC_CACHE[0]

    in_maps = []
    for c in range(8):
        b, h = c >> 1, c & 1
        in_maps.append({
            "x": x[b], "xown": np.ascontiguousarray(x[b, h * 2048:(h + 1) * 2048]),
            "wq": wq_h[h], "wk": wk_h[h], "wv": wv_h[h], "wo": wo,
            "wgf": wg_p, "wuf": wu_p, "wdf": wd_p,
            "costab": cos, "sintab": sin, "masks": msks,
            "identb": idb, "rotmat": rtm,
        })
    res = run_bass_kernel_spmd(nc, in_maps, core_ids=list(range(8)))
    out = np.empty((4, T, D), np.float32)
    for c in range(8):
        b, h = c >> 1, c & 1
        out[b, h * 2048:(h + 1) * 2048, :] = res.results[c]["out"]
    return out


# version 05 (final)
# speedup vs baseline: 1.2127x; 1.2127x over previous
import math
import numpy as np
import ml_dtypes
from contextlib import ExitStack

import concourse.bass as bass
import concourse.mybir as mybir
import concourse.bass_utils as _bu
from concourse.bass_utils import run_bass_kernel_spmd

if not getattr(_bu, "_ldw_opt_on", False):
    _orig_run_command = _bu.run_command

    def _run_command_ldw(argv, **kw):
        argv = ["--enable-ldw-opt=true" if a == "--enable-ldw-opt=false" else a
                for a in argv]
        return _orig_run_command(argv, **kw)

    _bu.run_command = _run_command_ldw
    _bu._ldw_opt_on = True

F32 = mybir.dt.float32
BF16 = mybir.dt.bfloat16
AF = mybir.ActivationFunctionType
ALU = mybir.AluOpType
ts = bass.ts
BF = ml_dtypes.bfloat16

D = 4096
T = 4096
DH = 128
NQH = 16
NKV = 4
NT = 32
NG = 8
DC = 32
F_REAL = 11008
F_PAD = 11264
FT = 88
EPS = 1e-5
EXP_SCALE = 1.0 / math.sqrt(DH)
EXP_BIAS = -20.0
CTX_OFF = (0, 132, 512, 644)
ENGS = ("pe", "dve", "act", "pool", "sp")
PAIRS = [[0, 1], [2, 3], [4, 5], [6, 7]]


class Prog:

    def __init__(self, cnt=None):
        self.events = []
        self.cnt = dict(cnt or {})

    def c(self, sem):
        return self.cnt.get(sem, 0)

    def ev(self, eng, fn, waits=(), inc=None, by=1):
        if inc is None and eng == "dve":
            inc = "sdv"
        if inc is None and eng == "act":
            inc = "sac"
        waits = [(s, v) for (s, v) in waits if v > 0]
        if eng == "dve" and fn is not None and self.c("sdv") > 0:
            waits.append(("sdv", self.c("sdv")))
        if eng == "act" and fn is not None and self.c("sac") > 0:
            waits.append(("sac", self.c("sac")))
        self.events.append((eng, fn, waits, inc, by))
        if inc:
            self.cnt[inc] = self.cnt.get(inc, 0) + by
        return self.cnt.get(inc, 0) if inc else 0

    def pe(self, fn, waits=(), inc=None):
        return self.ev("pe", fn, waits, inc)

    def dma(self, eng, fn, waits=(), inc="err"):
        return self.ev(eng, fn, waits, inc, 16)

    def fence(self, eng, waits):
        self.ev(eng, None, waits, inc=False)


def replay(prog, block, sems):
    def body_for(name):
        def body(eng):
            seen = {}
            for e, fn, waits, inc, by in prog.events:
                if e != name:
                    continue
                for s, v in waits:
                    if seen.get(s, -1) >= v:
                        continue
                    eng.wait_ge(sems[s], v)
                    seen[s] = v
                if fn is None:
                    continue
                ins = fn(eng)
                if inc:
                    ins.then_inc(sems[inc], by)
        return body
    block.tensor(body_for("pe"))
    block.vector(body_for("dve"))
    block.scalar(body_for("act"))
    block.gpsimd(body_for("pool"))
    block.sync(body_for("sp"))


_ECACHE = {}

I32 = mybir.dt.int32
RSQRT_MAGIC = 0x5F3759DF


def emit_rsqrt(p, var_ap, dst_ap, tmp):
    v = tmp[:, 0:1]
    y = tmp[:, 1:2]
    a = tmp[:, 2:3]
    p.ev("dve", lambda e: e.tensor_scalar_add(v, var_ap, EPS))
    p.ev("dve", lambda e: e.tensor_scalar(
        y.bitcast(I32), v.bitcast(I32), 1, None, op0=ALU.arith_shift_right))
    p.ev("dve", lambda e: e.tensor_scalar(
        y.bitcast(I32), y.bitcast(I32), RSQRT_MAGIC, -1,
        op0=ALU.subtract, op1=ALU.mult))
    for it in range(3):
        out_y = dst_ap if it == 2 else y
        p.ev("dve", lambda e: e.tensor_mul(a, y, y))
        p.ev("dve", lambda e: e.tensor_mul(a, a, v))
        p.ev("dve", lambda e: e.tensor_scalar(a, a, -0.5, 1.5, op0=ALU.mult, op1=ALU.add))
        p.ev("dve", (lambda out_y=out_y: lambda e: e.tensor_mul(out_y, y, a))())
    return p.c("sdv")


def build_nc():
    _ECACHE.clear()
    nc = bass.Bass("TRN2", target_bir_lowering=False)

    B = {}
    B["x_d"] = nc.dram_tensor("x", [T, D], F32, kind="ExternalInput").ap()
    B["wq_d"] = nc.dram_tensor("wq", [D, NQH * DH], F32, kind="ExternalInput").ap()
    B["wk_d"] = nc.dram_tensor("wk", [D, NKV * DH], F32, kind="ExternalInput").ap()
    B["wv_d"] = nc.dram_tensor("wv", [D, NKV * DH], F32, kind="ExternalInput").ap()
    B["wo_d"] = nc.dram_tensor("wo", [D, D], F32, kind="ExternalInput").ap()
    B["wgf_d"] = nc.dram_tensor("wgf", [F_PAD, D], F32, kind="ExternalInput").ap()
    B["wuf_d"] = nc.dram_tensor("wuf", [F_PAD, D], F32, kind="ExternalInput").ap()
    B["wdf_d"] = nc.dram_tensor("wdf", [D, F_PAD], F32, kind="ExternalInput").ap()
    B["xo_d"] = nc.dram_tensor("xown", [2048, D], F32, kind="ExternalInput").ap()
    B["cos_d"] = nc.dram_tensor("costab", [DH, T], BF16, kind="ExternalInput").ap()
    B["sin_d"] = nc.dram_tensor("sintab", [DH, T], BF16, kind="ExternalInput").ap()
    B["msk_d"] = nc.dram_tensor("masks", [128, 896], BF16, kind="ExternalInput").ap()
    B["idb_d"] = nc.dram_tensor("identb", [128, 128], BF16, kind="ExternalInput").ap()
    B["rt_d"] = nc.dram_tensor("rotmat", [128, 128], BF16, kind="ExternalInput").ap()
    B["out_d"] = nc.dram_tensor("out", [2048, D], F32, kind="ExternalOutput").ap()

    B["ctx_shr"] = nc.dram_tensor("ctx_shr", [2 * T, NQH * DH], BF16,
                                  addr_space="Shared", kind="Internal").ap()
    B["scr_g"] = nc.dram_tensor("scr_g", [128, FT * DC * 128], BF16, kind="Internal").ap()
    B["scr_u"] = nc.dram_tensor("scr_u", [128, FT * DC * 128], BF16, kind="Internal").ap()
    B["scr_d"] = nc.dram_tensor("scr_d", [128, 4 * FT * 1024], BF16, kind="Internal").ap()
    B["b1i"] = nc.dram_tensor("b1i", [128, 1], F32, kind="Internal").ap()
    B["b1o"] = nc.dram_tensor("b1o", [128, 1], F32, kind="Internal").ap()

    sem_names = ["scs", "sz", "sx0", "sx1", "sco", "ssi",
                 "swk0", "swk1", "swv0", "swv1", "swq0", "swq1",
                 "scw", "sbw0", "sbw1", "sbl0", "sbl1", "scc1",
                 "spe", "sdv", "sac", "sctx", "sxs", "s2r", "s2w0", "s2w1",
                 "s3g0", "s3g1", "srb0", "srb1", "srs", "s3w0", "s3w1", "syw"]

    with ExitStack() as outer:
        E = outer.enter_context
        B["consts"] = E(nc.sbuf_tensor("consts", [128, 6], F32))
        B["idb"] = E(nc.sbuf_tensor("idb", [128, 128], BF16))
        B["rt"] = E(nc.sbuf_tensor("rt", [128, 128], BF16))
        B["msk"] = E(nc.sbuf_tensor("msk", [128, 896], BF16))
        B["zero8"] = E(nc.sbuf_tensor("zero8", [128, 1], F32))
        B["slin"] = [E(nc.sbuf_tensor(f"slin{i}", [128, 2048], BF16)) for i in range(2)]
        B["stg"] = [E(nc.sbuf_tensor(f"stg{i}", [128, 2048], BF16)) for i in range(2)]
        sems = {n: E(nc.semaphore(n)) for n in sem_names}

        for j, v in enumerate([1.0, 0.0, EXP_BIAS, EXP_SCALE, EPS, -0.5]):
            nc.const_aps.register(F32, v, B["consts"][:, j:j + 1])

        units = make_units(B)

        with ExitStack() as st1:
            E1 = st1.enter_context
            B["K"] = E1(nc.sbuf_tensor("K", [128, NKV * T], BF16))
            B["Vh"] = E1(nc.sbuf_tensor("Vh", [128, NT * NKV * 129], BF16))
            B["LNXT"] = E1(nc.sbuf_tensor("LNXT", [128, DC * 512], BF16))
            B["QT"] = E1(nc.sbuf_tensor("QT", [128, NQH * 512], BF16))
            B["xbuf"] = [E1(nc.sbuf_tensor(f"xbuf{i}", [128, D], F32)) for i in range(2)]
            B["lnb"] = E1(nc.sbuf_tensor("lnb", [128, D], BF16))
            B["wqs"] = [E1(nc.sbuf_tensor(f"wqs{i}", [128, 2048], BF16)) for i in range(2)]
            B["wks"] = [E1(nc.sbuf_tensor(f"wks{i}", [128, 512], BF16)) for i in range(2)]
            B["wvs"] = [E1(nc.sbuf_tensor(f"wvs{i}", [128, 512], BF16)) for i in range(2)]
            B["cosg"] = E1(nc.sbuf_tensor("cosg", [128, 512], BF16))
            B["sing"] = E1(nc.sbuf_tensor("sing", [128, 512], BF16))
            B["praw"] = [E1(nc.sbuf_tensor("praw0", [128, 512], BF16))] * 2
            B["t1b"] = [E1(nc.sbuf_tensor("t1b0", [128, 512], BF16))] * 2
            B["t2b"] = [E1(nc.sbuf_tensor("t2b0", [128, 512], BF16))] * 2
            B["Pt"] = [E1(nc.sbuf_tensor(f"Pt{i}", [128, 1024], BF16)) for i in range(2)]
            B["cstg"] = E1(nc.sbuf_tensor("cstg", [128, 4 * NQH * DH], BF16))
            B["bnb"] = E1(nc.sbuf_tensor("bnb", [128, 48], F32))
            B["agb"] = E1(nc.sbuf_tensor("agb", [128, 2], F32))
            B["rstd"] = E1(nc.sbuf_tensor("rstd", [128, 1], F32))
            B["lnv"] = E1(nc.sbuf_tensor("lnv", [128, 4], F32))
            B["rtmp"] = B["lnv"]
            B["P"] = [E1(nc.psum_tensor(f"P{i}", [128, 1024], F32)) for i in range(4)]

            p1 = Prog()
            gen_phase1(p1, B, units)
            finals1 = [(s, p1.c(s)) for s in
                       ("spe", "sdv", "sac", "scw", "sbw0", "sbw1", "sx0", "sx1", "scs",
                        "sco", "ssi", "swk0", "swk1", "swv0", "swv1",
                        "swq0", "swq1", "sbl0", "sbl1", "sz")]
            with nc.Block("ph1") as blk1:
                replay(p1, blk1, sems)

        B["stats2"] = E(nc.sbuf_tensor("stats2", [128, 32], F32))
        with ExitStack() as st2:
            E2 = st2.enter_context
            B["ctxT"] = E2(nc.sbuf_tensor("ctxT", [128, 8 * DC * 128], BF16))
            B["ctxn8"] = E2(nc.sbuf_tensor("ctxn8", [128, 8 * D], BF16))
            B["wos"] = [E2(nc.sbuf_tensor(f"wos{i}", [128, 8192], BF16)) for i in range(2)]
            B["xst"] = E2(nc.sbuf_tensor("xst", [128, 2048], F32))
            B["rst"] = E2(nc.sbuf_tensor("rst", [128, 2048], F32))
            B["bn2"] = E2(nc.sbuf_tensor("bn2", [128, 768], F32))
            B["ag2"] = E2(nc.sbuf_tensor("ag2", [128, 2], F32))
            B["lnv2"] = E2(nc.sbuf_tensor("lnv2", [128, 4], F32))
            B["P2m"] = [E2(nc.psum_tensor(f"P2m{i}", [128, 1024], F32)) for i in range(2)]
            B["PT2"] = [E2(nc.psum_tensor(f"PT2{i}", [128, 1024], F32)) for i in range(2)]

            p2 = Prog(p1.cnt)
            for e in ENGS:
                p2.fence(e, finals1)
            gen_phase2(p2, B, units)
            finals2 = [(s, p2.c(s)) for s in
                       ("spe", "sdv", "sac", "sctx", "sxs", "s2r", "s2w0", "s2w1",
                        "sbw0", "sbw1", "sbl0", "sbl1", "scc1")]
            with nc.Block("ph2") as blk2:
                replay(p2, blk2, sems)

        with ExitStack() as st3:
            E3 = st3.enter_context
            B["hT"] = E3(nc.sbuf_tensor("hT", [128, FT * 512], BF16))
            B["LN2T"] = E3(nc.sbuf_tensor("LN2T", [128, DC * 512], BF16))
            B["silb"] = E3(nc.sbuf_tensor("silb", [128, 512], BF16))
            B["P3p"] = [E3(nc.psum_tensor(f"P3p{i}", [128, 1024], F32)) for i in range(4)]

            p3 = Prog(p2.cnt)
            for e in ENGS:
                p3.fence(e, finals2)
            gen_phase3(p3, B, nc)
            with nc.Block("ph3") as blk3:
                replay(p3, blk3, sems)

    return nc


def make_units(B):
    units = []
    for src, dst in ((B["wgf_d"], B["scr_g"]), (B["wuf_d"], B["scr_u"])):
        for u in range(FT):
            for hf in range(2):
                units.append(("gu", src, dst, (u, hf)))
    for sq in range(8):
        for fq in range(22):
            units.append(("d", None, None, (sq, fq)))
    return units


def gen_units(p, B, units, st, n):
    idb, slin, stg = B["idb"], B["slin"], B["stg"]
    for _ in range(n):
        if st["ui"] >= len(units):
            return
        kind, src, dst, u = units[st["ui"]]
        sl = st["ui"] % 2
        st["ui"] += 1
        if kind == "gu":
            u, hf = u
            rd = p.dma("pool", (lambda src=src, u=u, hf=hf, sl=sl: lambda e: e.dma_start(
                out=slin[sl][:, :],
                in_=src[u * 128:(u + 1) * 128, hf * 2048:(hf + 1) * 2048]))(),
                waits=[("spe", st["tp"][sl])], inc=f"sbl{sl}")
            nt = 16
        else:
            sq, fq = u
            rd = p.dma("pool", (lambda sq=sq, fq=fq, sl=sl: lambda e: e.dma_start(
                out=slin[sl][:, :].rearrange("p (j f) -> p j f", j=4),
                in_=B["wdf_d"][sq * 512:(sq + 1) * 512, fq * 512:(fq + 1) * 512]
                    .rearrange("(j p) f -> p j f", p=128)))(),
                waits=[("spe", st["tp"][sl])], inc=f"sbl{sl}")
            nt = 16
        ci = 0
        r = 0
        while ci < nt:
            nfill = min(4, nt - ci)
            pb = r % 2
            PT = st["PT"][pb]
            w = [("sdv", st["pbt"][pb])] + list(st.get("xw", []))
            if r == 0:
                w += [(f"sbl{sl}", rd), (f"sbw{sl}", st["wr"][sl])]
            for q in range(nfill):
                c = ci + q
                inc = "spe" if q == nfill - 1 else None
                p.pe((lambda sl=sl, c=c, q=q, PT=PT: lambda e: e.matmul(
                    PT[:, q * 128:(q + 1) * 128],
                    slin[sl][:, c * 128:(c + 1) * 128], idb[:, :],
                    start=True, stop=True))(),
                    waits=w if q == 0 else [], inc=inc)
            tv = p.c("spe")
            if kind == "gu":
                st["pbt"][pb] = p.ev("dve", (lambda sl=sl, ci=ci, PT=PT, nfill=nfill:
                                             lambda e: e.tensor_copy(
                    stg[sl][:, ci * 128:(ci + nfill) * 128],
                    PT[:, 0:nfill * 128]))(), waits=[("spe", tv)])
            else:
                st["pbt"][pb] = p.ev("dve", (lambda sl=sl, r=r, PT=PT:
                                             lambda e: e.tensor_copy(
                    stg[sl][:, :].rearrange("p (i f) -> p i f", i=4)
                        [:, :, r * 128:(r + 1) * 128],
                    PT[:, 0:512].rearrange("p (i f) -> p i f", i=4)))(),
                    waits=[("spe", tv)])
            ci += nfill
            r += 1
        st["xw"] = []
        st["tp"][sl] = p.c("spe")
        sev = p.c("sdv")
        if kind == "gu":
            c0 = (u * 32 + hf * 16) * 128
            st["wr"][sl] = p.dma("act", (lambda dst=dst, c0=c0, sl=sl:
                                         lambda e: e.dma_start(
                out=dst[:, c0:c0 + 2048], in_=stg[sl][:, :]))(),
                waits=[("sdv", sev)], inc=f"sbw{sl}")
        else:
            dsta = B["scr_d"].rearrange("p (fc d) -> p fc d", d=4096)
            st["wr"][sl] = p.dma("act", (lambda dsta=dsta, sq=sq, fq=fq, sl=sl:
                                         lambda e: e.dma_start(
                out=dsta[:, fq * 4:fq * 4 + 4, sq * 512:(sq + 1) * 512],
                in_=stg[sl][:, :].rearrange("p (i f) -> p i f", i=4)))(),
                waits=[("sdv", sev)], inc=f"sbw{sl}")


def gen_phase1(p, B, units):
    x_d, wq_d, wk_d, wv_d = B["x_d"], B["wq_d"], B["wk_d"], B["wv_d"]
    cos_d, sin_d = B["cos_d"], B["sin_d"]
    consts, idb, rt, msk, zero8 = B["consts"], B["idb"], B["rt"], B["msk"], B["zero8"]
    K, Vh, LNXT, QT = B["K"], B["Vh"], B["LNXT"], B["QT"]
    xbuf, lnb, wqs, wks, wvs = B["xbuf"], B["lnb"], B["wqs"], B["wks"], B["wvs"]
    cosg, sing, praw, t1b, t2b = B["cosg"], B["sing"], B["praw"], B["t1b"], B["t2b"]
    Pt, cstg = B["Pt"], B["cstg"]
    bnb, agb, rstd, lnv4, rtmp = B["bnb"], B["agb"], B["rstd"], B["lnv"], B["rtmp"]
    P = B["P"]
    ctx_shr = B["ctx_shr"]

    ustate = {"ui": 0, "tp": [0, 0], "pbt": [0, 0], "wr": [0, 0], "xw": [],
              "PT": [P[3][:, 0:512], P[3][:, 512:1024]]}

    for j, v in enumerate([1.0, 0.0, EXP_BIAS, EXP_SCALE, EPS, -0.5]):
        p.ev("dve", (lambda j=j, v=v: lambda e: e.memset(consts[:, j:j + 1], v))())
    p.ev("dve", lambda e: e.memset(zero8[:, :], 0.0))
    v_init = p.ev("dve", lambda e: e.memset(Vh[:, :], 1.0))
    p.dma("sp", lambda e: e.dma_start(out=msk[:, :], in_=B["msk_d"]), inc="scs")
    p.dma("sp", lambda e: e.dma_start(out=idb[:, :], in_=B["idb_d"]), inc="scs")
    c_init = p.dma("sp", lambda e: e.dma_start(out=rt[:, :], in_=B["rt_d"]), inc="scs")
    p.dma("pool", lambda e: e.dma_start(out=B["b1i"], in_=zero8[:, :]),
          waits=[("sdv", v_init)], inc="sz")
    p.ev("act", lambda e: e.preload_act_table(AF.Exp))

    LNXT_r = LNXT[:, :].rearrange("p (c f) -> p c f", c=DC)
    Vh_r = Vh[:, :].rearrange("p (b f) -> p b f", f=129)

    tp_done = [0] * (NT + 2)
    ap_done = [0] * (NT + 2)
    x_val = [0] * (NT + 2)
    evac_val = 0
    scw_val = 0
    rope_done = 0
    q_done = 0

    def emit_x(t):
        if t >= NT:
            return
        w = [("sdv", ap_done[t - 2])] if t >= 2 else []
        x_val[t] = p.dma("sp", (lambda t=t: lambda e: e.dma_start(
            out=xbuf[t % 2][:, :], in_=x_d[t * 128:(t + 1) * 128, :]))(), waits=w,
            inc=f"sx{t % 2}")

    emit_x(0)
    emit_x(1)

    for gi in range(NG):
        t0 = 4 * gi
        kmax = t0 + 4

        co_val = p.dma("sp", (lambda gi=gi: lambda e: e.dma_start(
            out=cosg[:, :], in_=cos_d[:, gi * 512:(gi + 1) * 512]))(),
            waits=[("sdv", rope_done)], inc="sco")
        cs_val = p.dma("sp", (lambda gi=gi: lambda e: e.dma_start(
            out=sing[:, :], in_=sin_d[:, gi * 512:(gi + 1) * 512]))(), inc="ssi")

        for j in range(4):
            t = t0 + j
            for ch in range(8):
                w = [(f"sx{t % 2}", x_val[t])] if ch == 0 else []
                p.ev("dve", (lambda t=t, ch=ch: lambda e: e.bn_stats(
                    bnb[:, ch * 6:(ch + 1) * 6],
                    xbuf[t % 2][:, ch * 512:(ch + 1) * 512]))(), waits=w)
            ag_val = p.ev("dve", lambda e: e.bn_aggr(agb[:, :], bnb[:, :]),
                          waits=[("sdv", p.c("sdv"))])
            rs_val = emit_rsqrt(p, agb[:, 1:2], rstd[:, 0:1], lnv4)
            ap_done[t] = p.ev("dve", (lambda t=t: lambda e: e.tensor_scalar(
                lnb[:, :], xbuf[t % 2][:, :], agb[:, 0:1], rstd[:, 0:1],
                op0=ALU.subtract, op1=ALU.mult))(),
                waits=[("sdv", rs_val), ("spe", tp_done[t - 1] if t else 0)])
            for r in range(8):
                hb = r % 2
                w = [("sdv", ustate["pbt"][hb])]
                if r == 0:
                    w += [("sdv", ap_done[t]), ("scs", c_init)]
                for q in range(4):
                    c = 4 * r + q
                    inc = "spe" if q == 3 else None
                    p.pe((lambda t=t, c=c, hb=hb, q=q: lambda e: e.matmul(
                        P[3][:, hb * 512 + q * 128:hb * 512 + (q + 1) * 128],
                        lnb[:, c * 128:(c + 1) * 128], idb[:, :],
                        start=True, stop=True))(),
                        waits=w if q == 0 else [], inc=inc)
                tv = p.c("spe")
                src = P[3][:, hb * 512:(hb + 1) * 512].rearrange("p (c f) -> p c f", c=4)
                ustate["pbt"][hb] = p.ev("dve", (lambda r=r, j=j, src=src: lambda e: e.tensor_copy(
                    LNXT_r[:, 4 * r:4 * r + 4, j * 128:(j + 1) * 128], src))(),
                    waits=[("spe", tv)])
            tp_done[t] = p.c("spe")
            emit_x(t + 2)
        lnxt_val = p.c("sdv")

        def kv_dma(c, w):
            p.dma("pool", (lambda c=c: lambda e: e.dma_start(
                out=wks[c % 2][:, :], in_=wk_d[c * 128:(c + 1) * 128, :]))(),
                waits=w, inc=f"swk{c % 2}")
            p.dma("pool", (lambda c=c: lambda e: e.dma_start(
                out=wvs[c % 2][:, :], in_=wv_d[c * 128:(c + 1) * 128, :]))(),
                inc=f"swv{c % 2}")
            return [(f"swk{c % 2}", p.c(f"swk{c % 2}")), (f"swv{c % 2}", p.c(f"swv{c % 2}"))]

        kv_rd = [None] * DC
        kv_rd[0] = kv_dma(0, [("spe", p.c("spe"))])
        kv_rd[1] = kv_dma(1, [])
        for c in range(DC):
            w = list(kv_rd[c])
            if c == 0:
                w += [("sdv", lnxt_val), ("sdv", evac_val), ("sdv", v_init),
                      ("sdv", ustate["pbt"][0]), ("sdv", ustate["pbt"][1])]
            for kv in range(4):
                Pk, sl = P[kv // 2], (kv % 2) * 512
                p.pe((lambda c=c, kv=kv, Pk=Pk, sl=sl: lambda e: e.matmul(
                    Pk[:, sl:sl + 512],
                    wks[c % 2][:, kv * 128:(kv + 1) * 128],
                    LNXT[:, c * 512:(c + 1) * 512],
                    start=(c == 0), stop=(c == DC - 1)))(),
                    waits=w if kv == 0 else [])
            for j in range(4):
                Pv, sl = P[2 + j // 2], (j % 2) * 512
                inc = "spe" if j == 3 else None
                p.pe((lambda c=c, j=j, Pv=Pv, sl=sl: lambda e: e.matmul(
                    Pv[:, sl:sl + 512],
                    LNXT[:, c * 512 + j * 128:c * 512 + (j + 1) * 128],
                    wvs[c % 2][:, :],
                    start=(c == 0), stop=(c == DC - 1)))(), inc=inc)
            if c + 2 < DC:
                kv_rd[c + 2] = kv_dma(c + 2, [("spe", p.c("spe"))])
        kv_done = p.c("spe")

        vev = []
        for j in range(4):
            Pv, sl = P[2 + j // 2], (j % 2) * 512
            src = Pv[:, sl:sl + 512].rearrange("p (b f) -> p b f", f=128)
            dst = Vh_r[:, (t0 + j) * 4:(t0 + j) * 4 + 4, 0:128]
            vev.append(p.ev("dve", (lambda src=src, dst=dst: lambda e: e.tensor_copy(dst, src))(),
                            waits=[("spe", kv_done)]))

        kq_last = [0, 0]
        rot_prev = 0
        for j in range(4):
            Pk, sl = P[j // 2], (j % 2) * 512
            w = [("spe", kv_done), ("spe", rot_prev)]
            if j >= 2:
                w.append(("sdv", kq_last[j % 2]))
            cp = p.ev("dve", (lambda Pk=Pk, sl=sl, j=j: lambda e: e.tensor_copy(
                praw[j % 2][:, :], Pk[:, sl:sl + 512]))(), waits=w)
            rw = [("sdv", cp), ("sdv", vev[j])]
            if j >= 2:
                rw.append(("sdv", ustate["pbt"][j % 2]))
            rot = p.pe((lambda j=j: lambda e: e.matmul(
                P[2 + j // 2][:, (j % 2) * 512:(j % 2) * 512 + 512],
                rt[:, :], praw[j % 2][:, :], start=True, stop=True))(),
                waits=rw, inc="spe")
            rot_prev = rot
            p.ev("dve", (lambda j=j: lambda e: e.tensor_mul(
                t1b[0][:, :], praw[j % 2][:, :], cosg[:, :]))(),
                waits=[("sco", co_val), ("ssi", cs_val), ("sdv", kq_last[(j + 1) % 2])])
            p.ev("dve", (lambda j=j: lambda e: e.tensor_mul(
                t2b[0][:, :], P[2 + j // 2][:, (j % 2) * 512:(j % 2) * 512 + 512],
                sing[:, :]))(), waits=[("spe", rot)])
            kq_last[j % 2] = p.ev("dve", (lambda j=j, t0=t0: lambda e: e.tensor_add(
                K[:, j * T + t0 * 128:j * T + t0 * 128 + 512],
                t1b[0][:, :], t2b[0][:, :]))(),
                waits=[("sdv", p.c("sdv"))])
        k_done = p.c("sdv")

        for half in range(2):
            def q_dma(bi, w, half=half):
                srcsl = wq_d[bi * 256:(bi + 1) * 256, half * 1024:(half + 1) * 1024]
                srca = srcsl.rearrange("(c p) f -> p c f", p=128)
                dsta = wqs[bi % 2][:, :].rearrange("p (c f) -> p c f", c=2)
                p.dma("pool", (lambda srca=srca, dsta=dsta: lambda e: e.dma_start(
                    out=dsta, in_=srca))(), waits=w, inc=f"swq{bi % 2}")
                return (f"swq{bi % 2}", p.c(f"swq{bi % 2}"))

            q_rd = [None] * 16
            q_rd[0] = q_dma(0, [("spe", p.c("spe"))])
            q_rd[1] = q_dma(1, [])
            for c in range(DC):
                bi = c // 2
                w = [q_rd[bi]]
                if c == 0:
                    w.append(("sdv", k_done if half == 0 else q_done))
                for k8 in range(8):
                    Pq, sl = P[k8 // 2], (k8 % 2) * 512
                    inc = "spe" if (c % 2 == 1 and k8 == 7) else None
                    p.pe((lambda c=c, k8=k8, half=half, Pq=Pq, sl=sl: lambda e: e.matmul(
                        Pq[:, sl:sl + 512],
                        wqs[(c // 2) % 2][:, (c % 2) * 1024 + k8 * 128:(c % 2) * 1024 + (k8 + 1) * 128],
                        LNXT[:, c * 512:(c + 1) * 512],
                        start=(c == 0), stop=(c == DC - 1)))(),
                        waits=w if k8 == 0 else [], inc=inc)
                if c % 2 == 1 and c // 2 + 2 < 16:
                    q_rd[c // 2 + 2] = q_dma(c // 2 + 2, [("spe", p.c("spe"))])
            qproj = p.c("spe")
            rot_last = [0, 0]
            rot_prev = 0
            for k8 in range(8):
                qh = half * 8 + k8
                Pq, sl = P[k8 // 2], (k8 % 2) * 512
                w = [("spe", qproj), ("spe", rot_prev)]
                if k8 >= 2:
                    w.append(("sdv", rot_last[k8 % 2]))
                cp = p.ev("dve", (lambda Pq=Pq, sl=sl, k8=k8: lambda e: e.tensor_copy(
                    praw[k8 % 2][:, :], Pq[:, sl:sl + 512]))(), waits=w)
                rw = [("sdv", cp)]
                if k8 >= 6:
                    rw.append(("sdv", ustate["pbt"][k8 % 2]))
                rot = p.pe((lambda Pq=Pq, sl=sl, k8=k8: lambda e: e.matmul(
                    Pq[:, sl:sl + 512], rt[:, :], praw[k8 % 2][:, :],
                    start=True, stop=True))(), waits=rw, inc="spe")
                rot_prev = rot
                p.ev("dve", (lambda k8=k8: lambda e: e.tensor_mul(
                    t1b[0][:, :], praw[k8 % 2][:, :], cosg[:, :]))(),
                    waits=[("sdv", rot_last[(k8 + 1) % 2])])
                p.ev("dve", (lambda Pq=Pq, sl=sl, k8=k8: lambda e: e.tensor_mul(
                    t2b[0][:, :], Pq[:, sl:sl + 512], sing[:, :]))(),
                    waits=[("spe", rot)])
                rot_last[k8 % 2] = p.ev("dve", (lambda qh=qh, k8=k8: lambda e: e.tensor_add(
                    QT[:, qh * 512:(qh + 1) * 512], t1b[0][:, :], t2b[0][:, :]))(),
                    waits=[("sdv", p.c("sdv"))])
            q_done = p.c("sdv")
        rope_done = q_done

        pairs = [(h, pp) for h in range(NQH) for pp in range(kmax // 2)]
        npr = len(pairs)
        s_done = [0] * npr
        exp_dep = [None] * npr
        pv_done = [0] * npr
        s_read = [0, 0]

        def emit_S(i):
            h, pp = pairs[i]
            par = i % 2
            kvh = h // 4
            kc0 = 2 * pp
            w = [("sac", s_read[par])]
            if i == 0:
                w.append(("sdv", q_done))
            for ii in range(2):
                kc = kc0 + ii
                inc = "spe" if ii == 1 else None
                p.pe((lambda kvh=kvh, kc=kc, h=h, par=par, ii=ii: lambda e: e.matmul(
                    P[par][:, ii * 512:(ii + 1) * 512],
                    K[:, kvh * T + kc * 128:kvh * T + (kc + 1) * 128],
                    QT[:, h * 512:(h + 1) * 512], start=True, stop=True))(),
                    waits=w if ii == 0 else [], inc=inc)
            s_done[i] = p.c("spe")

        def emit_exp(i):
            h, pp = pairs[i]
            par = i % 2
            kc0 = 2 * pp
            ev = p.ev("act", (lambda par=par: lambda e: e.activation(
                Pt[par][:, :], P[par][:, :], AF.Exp,
                bias=EXP_BIAS, scale=EXP_SCALE))(),
                waits=[("spe", s_done[i]),
                       ("spe", pv_done[i - 2] if i >= 2 else 0)])
            s_read[par] = ev
            dep = ("sac", ev)
            if kc0 >= t0:
                for ii in range(2):
                    mj = kc0 + ii - t0
                    mv = p.ev("dve", (lambda par=par, ii=ii, mj=mj: lambda e: e.tensor_mul(
                        Pt[par][:, ii * 512:(ii + 1) * 512],
                        Pt[par][:, ii * 512:(ii + 1) * 512],
                        msk[:, 384 - 128 * mj:896 - 128 * mj]))(),
                        waits=[("sac", ev)])
                dep = ("sdv", mv)
            exp_dep[i] = dep

        def emit_evac(h):
            nonlocal evac_val
            pvfin = pv_done[(h + 1) * (kmax // 2) - 1]
            for jj in range(4):
                o = CTX_OFF[jj]
                w = [("spe", pvfin)]
                if h == 0:
                    w.append(("scw", scw_val))
                rv = p.ev("dve", (lambda o=o, jj=jj: lambda e: e.reciprocal(
                    rtmp[:, jj:jj + 1], P[2][:, o + 128:o + 129]))(), waits=w)
                evac_val = p.ev("dve", (lambda o=o, jj=jj, h=h: lambda e: e.tensor_scalar(
                    cstg[:, jj * 2048 + h * 128:jj * 2048 + (h + 1) * 128],
                    P[2][:, o:o + 128],
                    rtmp[:, jj:jj + 1], None, op0=ALU.mult))(),
                    waits=[("sdv", rv)])
            if h == 7:
                ustate["xw"] = [("sdv", q_done)]
                gen_units(p, B, units, ustate, 14)

        def emit_PV(i):
            h, pp = pairs[i]
            par = i % 2
            kvh = h // 4
            kc0 = 2 * pp
            if pp == 0 and h > 0:
                emit_evac(h - 1)
            w = [exp_dep[i]]
            if pp == 0:
                w.append(("sdv", evac_val))
            for ii in range(2):
                kc = kc0 + ii
                for jj in range(4):
                    o = CTX_OFF[jj]
                    inc = "spe" if (ii == 1 and jj == 3) else None
                    st_f = (kc == 0 and jj % 2 == 0)
                    sp_f = (kc == kmax - 1 and jj % 2 == 1)
                    p.pe((lambda par=par, ii=ii, jj=jj, o=o, kc=kc, kvh=kvh,
                                 st_f=st_f, sp_f=sp_f:
                          lambda e: e.matmul(
                              P[2][:, o:o + 129],
                              Pt[par][:, ii * 512 + jj * 128:ii * 512 + (jj + 1) * 128],
                              Vh[:, (kc * 4 + kvh) * 129:(kc * 4 + kvh) * 129 + 129],
                              start=st_f, stop=sp_f))(),
                        waits=w if (ii == 0 and jj == 0) else [], inc=inc)
            pv_done[i] = p.c("spe")

        for i in range(npr):
            emit_S(i)
            if i >= 1:
                emit_exp(i - 1)
            if i >= 2:
                emit_PV(i - 2)
        emit_exp(npr - 1)
        for i in range(npr - 2, npr):
            emit_PV(i)
        emit_evac(NQH - 1)

        def ctx_write(e, tt):
            k = (id(e), "ctx")
            if k not in _ECACHE:
                pid = e.snap(e.partition_id() % 2, min_val=0, max_val=1)
                base = e.snap(pid * (T * NQH * DH), min_val=0, max_val=T * NQH * DH)
                reg = e.alloc_register("ctxoff")
                _ECACHE[k] = (base, reg)
            base, reg = _ECACHE[k]
            e.reg_alu(reg, base, tt * 128 * 2048, ALU.add)
            out_ap = bass.AP(ctx_shr.tensor, reg,
                             [[2048, 128], [128 * 2048, 4], [1, 2048]])
            return e.dma_start(out=out_ap,
                               in_=cstg[:, :].rearrange("p (t f) -> p t f", t=4))
        scw_val = p.dma("sp", (lambda tt=t0: lambda e: ctx_write(e, tt))(),
                        waits=[("sdv", evac_val)], inc="scw")
        ustate["xw"] = [("sdv", q_done)]
        gen_units(p, B, units, ustate, 14)

    p.ev("pool", lambda e: e.collective_compute(
        "AllReduce", ALU.add, replica_groups=PAIRS,
        ins=[B["b1i"].opt()], outs=[B["b1o"].opt()]),
        waits=[("scw", p.c("scw")), ("sdv", p.c("sdv")), ("sz", p.c("sz"))],
        inc="scc1")
    B["_ustate"] = ustate


def gen_phase2(p, B, units):
    xo_d, wo_d, out_d, ctx_shr = B["xo_d"], B["wo_d"], B["out_d"], B["ctx_shr"]
    idb, stats2 = B["idb"], B["stats2"]
    ctxT, ctxn8, wos, xst, rst = B["ctxT"], B["ctxn8"], B["wos"], B["xst"], B["rst"]
    bn2, ag2, lnv2 = B["bn2"], B["ag2"], B["lnv2"]
    P2m, PT2 = B["P2m"], B["PT2"]
    ustate = B["_ustate"]
    ustate["PT"] = [PT2[1][:, 0:512], PT2[1][:, 512:1024]]
    ustate["xw"] = []

    for e in ENGS:
        p.fence(e, [("scc1", 1)])

    xo_r = xo_d.rearrange("(r p) f -> p r f", p=128)
    osrc = out_d.rearrange("(r p) f -> p r f", p=128)

    def ctx_read(e, sl2, q8):
        k = (id(e), "ctxr")
        if k not in _ECACHE:
            pid = e.snap(e.partition_id() % 2, min_val=0, max_val=1)
            base = e.snap(pid * (2048 * 2048), min_val=0, max_val=2048 * 2048)
            reg = e.alloc_register("ctxroff")
            _ECACHE[k] = (base, reg)
        base, reg = _ECACHE[k]
        e.reg_alu(reg, base, (sl2 * T + q8 * 1024) * 2048, ALU.add)
        in_ap = bass.AP(ctx_shr.tensor, reg,
                        [[2048, 128], [128 * 2048, 8], [1, 2048]])
        dst = ctxn8[:, :].rearrange("p (r s f) -> p r s f", r=8, s=2)[:, :, sl2, :]
        return e.dma_start(out=dst, in_=in_ap)

    tpq_done = [0] * 18
    ctr2 = [0, 0]
    sx_mm = [0] * 40
    ev_slot = [0, 0, 0, 0]
    x_rd = 0
    r_wr = 0
    bn_done = 0
    for q8 in range(2):
        crd = 0
        for sl2 in range(2):
            crd = p.dma("sp", (lambda sl2=sl2, q8=q8: lambda e: ctx_read(e, sl2, q8))(),
                        waits=[("spe", tpq_done[7])] if (q8 and sl2 == 0) else [],
                        inc="sctx")
        for qt in range(8):
            qtg = q8 * 8 + qt
            for r in range(8):
                pb = r % 2
                w2 = [("sdv", ctr2[pb])]
                if r == 0 and qt == 0:
                    w2.append(("sctx", crd))
                for q in range(4):
                    c = 4 * r + q
                    inc = "spe" if q == 3 else None
                    p.pe((lambda qt=qt, c=c, pb=pb, q=q: lambda e: e.matmul(
                        PT2[0][:, pb * 512 + q * 128:pb * 512 + (q + 1) * 128],
                        ctxn8[:, qt * 4096 + c * 128:qt * 4096 + (c + 1) * 128],
                        idb[:, :], start=True, stop=True))(),
                        waits=w2 if q == 0 else [], inc=inc)
                tv = p.c("spe")
                ctr2[pb] = p.ev("dve", (lambda qt=qt, r=r, pb=pb: lambda e: e.tensor_copy(
                    ctxT[:, qt * 4096 + r * 512:qt * 4096 + (r + 1) * 512],
                    PT2[0][:, pb * 512:(pb + 1) * 512]))(), waits=[("spe", tv)])
            tpq_done[qtg] = p.c("spe")
        ctxT_all = p.c("sdv")

        for sx in range(16):
            sxg = q8 * 16 + sx
            wsl = sxg % 2
            for cb in range(4):
                srcsl = wo_d[cb * 1024:(cb + 1) * 1024, sx * 256:(sx + 1) * 256]
                srca = srcsl.rearrange("(c p) f -> p c f", p=128)
                dsta = wos[wsl][:, :].rearrange("p (c f) -> p c f", c=32)
                p.dma("pool", (lambda srca=srca, dsta=dsta, cb=cb: lambda e: e.dma_start(
                    out=dsta[:, cb * 8:(cb + 1) * 8, :], in_=srca))(),
                    waits=[("spe", sx_mm[sxg])] if cb == 0 else [], inc=f"s2w{wsl}")
            wo_val = p.c(f"s2w{wsl}")
            x_rd = p.dma("sp", (lambda q8=q8, sx=sx: lambda e: e.dma_start(
                out=xst[:, :].rearrange("p (r f) -> p r f", r=8),
                in_=xo_r[:, q8 * 8:q8 * 8 + 8, sx * 256:(sx + 1) * 256]))(),
                waits=[("sdv", bn_done)], inc="sxs")
            for qt in range(8):
                psl, off = qt % 2, ((qt // 2) % 2) * 512
                w = [(f"s2w{wsl}", wo_val), ("sdv", ctxT_all), ("sdv", ev_slot[qt % 4])]
                for c in range(DC):
                    inc = "spe" if c == DC - 1 else None
                    p.pe((lambda qt=qt, c=c, psl=psl, off=off, wsl=wsl: lambda e: e.matmul(
                        P2m[psl][:, off:off + 256],
                        ctxT[:, qt * 4096 + c * 128:qt * 4096 + (c + 1) * 128],
                        wos[wsl][:, c * 256:(c + 1) * 256],
                        start=(c == 0), stop=(c == DC - 1)))(),
                        waits=w if c == 0 else [], inc=inc)
                mmv = p.c("spe")
                ev_slot[qt % 4] = p.ev("dve", (lambda qt=qt, psl=psl, off=off:
                                               lambda e: e.tensor_add(
                    rst[:, qt * 256:(qt + 1) * 256], P2m[psl][:, off:off + 256],
                    xst[:, qt * 256:(qt + 1) * 256]))(),
                    waits=[("spe", mmv), ("sxs", x_rd), ("s2r", r_wr)])
                p.ev("dve", (lambda qt=qt, sx=sx: lambda e: e.bn_stats(
                    bn2[:, (qt * 16 + sx) * 6:(qt * 16 + sx + 1) * 6],
                    rst[:, qt * 256:(qt + 1) * 256]))())
            bn_done = p.c("sdv")
            sx_mm[sxg + 2] = p.c("spe")
            r_wr = p.dma("sp", (lambda q8=q8, sx=sx: lambda e: e.dma_start(
                out=osrc[:, q8 * 8:q8 * 8 + 8, sx * 256:(sx + 1) * 256],
                in_=rst[:, :].rearrange("p (r f) -> p r f", r=8)))(),
                waits=[("sdv", bn_done)], inc="s2r")
            gen_units(p, B, units, ustate, 5)
        for qt in range(8):
            qtg = q8 * 8 + qt
            p.ev("dve", (lambda qt=qt: lambda e: e.bn_aggr(
                ag2[:, :], bn2[:, qt * 96:(qt + 1) * 96]))(),
                waits=[("sdv", bn_done)])
            p.ev("dve", (lambda qtg=qtg: lambda e: e.tensor_copy(
                stats2[:, qtg * 2:qtg * 2 + 1], ag2[:, 0:1]))())
            emit_rsqrt(p, ag2[:, 1:2], stats2[:, qtg * 2 + 1:qtg * 2 + 2], lnv2)

    gen_units(p, B, units, ustate, 10 ** 6)


def gen_phase3(p, B, nc):
    out_d = B["out_d"]
    idb, stats2 = B["idb"], B["stats2"]
    hT, LN2T = B["hT"], B["LN2T"]
    silb = B["silb"]
    P3p = B["P3p"]

    p.ev("act", lambda e: e.preload_act_table(AF.Silu))

    LN2T_r = LN2T[:, :].rearrange("p (c f) -> p c f", c=DC)
    out_r = out_d.rearrange("(r p) f -> p r f", p=128)
    scr_g_r = B["scr_g"]
    scr_u_r = B["scr_u"]
    scr_d_r = B["scr_d"].rearrange("p (fc d) -> p fc d", d=4096)

    r_rd = [0, 0]
    tp_last = 0
    ev_round = [0, 0]
    ap_prev = 0
    wg_used = [0, 0]
    h_last = [0, 0]
    wd_used = [0, 0]
    y_wr = 0
    down_ev = 0
    for fg in range(4):
      with ExitStack() as sec1:
        rbuf = [sec1.enter_context(nc.sbuf_tensor(f"rbuf{fg}{i}", [128, D], F32))
                for i in range(2)]
        ln2b = sec1.enter_context(nc.sbuf_tensor(f"ln2b{fg}", [128, D], BF16))
        for j in range(4):
            qtg = fg * 4 + j
            w = [("s2r", p.c("s2r"))]
            if j >= 2 or fg > 0:
                w.append(("sdv", ap_prev))
            if j == 0 and fg > 0:
                w += [("sdv", down_ev), ("syw", y_wr)]
            r_rd[j % 2] = p.dma("sp", (lambda qtg=qtg, j=j: lambda e: e.dma_start(
                out=rbuf[j % 2][:, :], in_=out_d[qtg * 128:(qtg + 1) * 128, :]))(),
                waits=w, inc=f"srb{j % 2}")
            apv = p.ev("dve", (lambda qtg=qtg, j=j: lambda e: e.tensor_scalar(
                ln2b[:, :], rbuf[j % 2][:, :],
                stats2[:, qtg * 2:qtg * 2 + 1], stats2[:, qtg * 2 + 1:qtg * 2 + 2],
                op0=ALU.subtract, op1=ALU.mult))(),
                waits=[(f"srb{j % 2}", r_rd[j % 2]), ("spe", tp_last)])
            ap_prev = apv
            for r in range(8):
                hb = r % 2
                w2 = [("sdv", ev_round[hb])]
                if r == 0:
                    w2.append(("sdv", apv))
                if r == 0 and j == 0 and fg > 0:
                    w2.append(("sdv", down_ev))
                for q in range(4):
                    c = 4 * r + q
                    inc = "spe" if q == 3 else None
                    p.pe((lambda j=j, c=c, hb=hb, q=q: lambda e: e.matmul(
                        P3p[2][:, hb * 512 + q * 128:hb * 512 + (q + 1) * 128],
                        ln2b[:, c * 128:(c + 1) * 128], idb[:, :],
                        start=True, stop=True))(),
                        waits=w2 if q == 0 else [], inc=inc)
                tv = p.c("spe")
                src = P3p[2][:, hb * 512:(hb + 1) * 512].rearrange("p (c f) -> p c f", c=4)
                ev_round[hb] = p.ev("dve", (lambda r=r, j=j, src=src: lambda e: e.tensor_copy(
                    LN2T_r[:, 4 * r:4 * r + 4, j * 128:(j + 1) * 128], src))(),
                    waits=[("spe", tv)])
            tp_last = p.c("spe")
        ln2_done = p.c("sdv")

      with ExitStack() as sec2:
        wgs = [sec2.enter_context(nc.sbuf_tensor(f"wgs{fg}{i}", [128, 4096], BF16))
               for i in range(2)]
        wus = [sec2.enter_context(nc.sbuf_tensor(f"wus{fg}{i}", [128, 4096], BF16))
               for i in range(2)]
        for ft in range(FT):
            sl = ft % 2
            w0 = [("spe", wg_used[sl])]
            if ft < 2:
                w0.append(("spe", tp_last))
            p.dma("sp", (lambda ft=ft, sl=sl: lambda e: e.dma_start(
                out=wgs[sl][:, :],
                in_=scr_g_r[:, ft * 4096:(ft + 1) * 4096]))(),
                waits=w0, inc=f"s3g{sl}")
            wrd = p.dma("sp", (lambda ft=ft, sl=sl: lambda e: e.dma_start(
                out=wus[sl][:, :],
                in_=scr_u_r[:, ft * 4096:(ft + 1) * 4096]))(), inc=f"s3g{sl}")
            Pp = P3p[sl]
            w = [(f"s3g{sl}", wrd), ("sdv", h_last[sl])]
            if ft == 0:
                w.append(("sdv", ln2_done))
            for c in range(DC):
                inc = "spe" if c == DC - 1 else None
                p.pe((lambda c=c, sl=sl, Pp=Pp: lambda e: e.matmul(
                    Pp[:, 0:512], wgs[sl][:, c * 128:(c + 1) * 128],
                    LN2T[:, c * 512:(c + 1) * 512],
                    start=(c == 0), stop=(c == DC - 1)))(),
                    waits=w if c == 0 else [])
                p.pe((lambda c=c, sl=sl, Pp=Pp: lambda e: e.matmul(
                    Pp[:, 512:1024], wus[sl][:, c * 128:(c + 1) * 128],
                    LN2T[:, c * 512:(c + 1) * 512],
                    start=(c == 0), stop=(c == DC - 1)))(), inc=inc)
            mmv = p.c("spe")
            wg_used[sl] = mmv
            sv = p.ev("act", (lambda Pp=Pp: lambda e: e.activation(
                silb[:, :], Pp[:, 0:512], AF.Silu, bias=0.0, scale=1.0))(),
                waits=[("spe", mmv), ("sdv", h_last[1 - sl])])
            h_last[sl] = p.ev("dve", (lambda ft=ft, Pp=Pp: lambda e: e.tensor_mul(
                hT[:, ft * 512:(ft + 1) * 512], silb[:, :], Pp[:, 512:1024]))(),
                waits=[("sac", sv)])
        h_all = p.c("sdv")
        gate_fin = p.c("spe")

      with ExitStack() as sec3:
        wds = [sec3.enter_context(nc.sbuf_tensor(f"wds{fg}{i}", [128, 4096], BF16))
               for i in range(2)]
        rstg = sec3.enter_context(nc.sbuf_tensor(f"rstg{fg}", [128, 4096], F32))
        ystg = sec3.enter_context(nc.sbuf_tensor(f"ystg{fg}", [128, 4096], F32))
        for dq in range(4):
            w0 = [("syw", y_wr)]
            if dq == 0:
                w0 += [("spe", gate_fin), ("sdv", h_all)]
            rs_rd = p.dma("sp", (lambda fg=fg, dq=dq: lambda e: e.dma_start(
                out=rstg[:, :].rearrange("p (r f) -> p r f", r=4),
                in_=out_r[:, fg * 4:fg * 4 + 4, dq * 1024:(dq + 1) * 1024]))(),
                waits=w0, inc="srs")
            for k in range(22):
                sl = k % 2
                w1 = [("spe", wd_used[sl])]
                if dq == 0 and k < 2:
                    w1 += [("spe", gate_fin), ("sdv", h_all)]
                wd_rd = p.dma("act", (lambda dq=dq, k=k, sl=sl: lambda e: e.dma_start(
                    out=wds[sl][:, :].rearrange("p (i f) -> p i f", i=4),
                    in_=scr_d_r[:, k * 4:k * 4 + 4, dq * 1024:(dq + 1) * 1024]))(),
                    waits=w1, inc=f"s3w{sl}")
                w = [(f"s3w{sl}", wd_rd), ("sdv", h_all)]
                if k == 0:
                    w.append(("sdv", down_ev))
                for i in range(4):
                    fc = k * 4 + i
                    for qt in range(4):
                        for nn in range(2):
                            inc = "spe" if (i == 3 and qt == 3 and nn == 1) else None
                            p.pe((lambda fc=fc, qt=qt, nn=nn, i=i, sl=sl: lambda e: e.matmul(
                                P3p[qt][:, nn * 512:(nn + 1) * 512],
                                hT[:, fc * 512 + qt * 128:fc * 512 + (qt + 1) * 128],
                                wds[sl][:, i * 1024 + nn * 512:i * 1024 + (nn + 1) * 512],
                                start=(fc == 0), stop=(fc == FT - 1)))(),
                                waits=w if (i == 0 and qt == 0 and nn == 0) else [], inc=inc)
                wd_used[sl] = p.c("spe")
            dnv = p.c("spe")
            for qt in range(4):
                down_ev = p.ev("dve", (lambda qt=qt: lambda e: e.tensor_add(
                    ystg[:, qt * 1024:(qt + 1) * 1024], P3p[qt][:, :],
                    rstg[:, qt * 1024:(qt + 1) * 1024]))(),
                    waits=[("spe", dnv), ("srs", rs_rd)])
            y_wr = p.dma("sp", (lambda fg=fg, dq=dq: lambda e: e.dma_start(
                out=out_r[:, fg * 4:fg * 4 + 4, dq * 1024:(dq + 1) * 1024],
                in_=ystg[:, :].rearrange("p (r f) -> p r f", r=4)))(),
                waits=[("sdv", down_ev)], inc="syw")

    p.fence("sp", [("syw", p.c("syw"))])


def _rope_tables():
    idx = np.arange(0, DH, 2, dtype=np.float64) / DH
    freq = (1.0 / (2.0 * math.pi)) * (500000.0 ** (-idx))
    factor, lo, hi, L0 = 32.0, 1.0, 4.0, 8192.0
    freq_low, freq_high = lo / L0, hi / L0
    freq_scaled = np.where(freq < freq_low, freq / factor, freq)
    smooth = np.clip((L0 * freq - lo) / (hi - lo), 0.0, 1.0)
    freq_smooth = (1.0 - smooth) * (freq / factor) + smooth * freq
    is_mid = (freq >= freq_low) & (freq <= freq_high)
    fr = np.where(is_mid, freq_smooth, freq_scaled)
    t = np.arange(T, dtype=np.float64)
    phase = 2.0 * math.pi * t[:, None] * fr[None, :]
    emb = np.concatenate([phase, phase], axis=1)
    cos = np.ascontiguousarray(np.cos(emb).T).astype(BF)
    sin = np.ascontiguousarray(np.sin(emb).T).astype(BF)
    return cos, sin


def _masks():
    k = np.arange(128)[:, None]
    i = np.arange(896)[None, :]
    return (k <= (i - 384)).astype(BF)


def _rotmat():
    rt = np.zeros((128, 128), dtype=BF)
    for i in range(64):
        rt[i, i + 64] = 1.0
        rt[i + 64, i] = -1.0
    return rt


_NC_CACHE = []


def kernel(L0_attn_resid, L0_attn_Wq, L0_attn_Wk, L0_attn_Wv, L0_attn_Wo,
           L0_ffn_W_gate, L0_ffn_W_up, L0_ffn_W_down):
    x = np.ascontiguousarray(np.asarray(L0_attn_resid, dtype=np.float32))
    wq = np.ascontiguousarray(np.asarray(L0_attn_Wq, dtype=np.float32))
    wk = np.ascontiguousarray(np.asarray(L0_attn_Wk, dtype=np.float32))
    wv = np.ascontiguousarray(np.asarray(L0_attn_Wv, dtype=np.float32))
    wo = np.ascontiguousarray(np.asarray(L0_attn_Wo, dtype=np.float32))
    wg = np.asarray(L0_ffn_W_gate, dtype=np.float32)
    wu = np.asarray(L0_ffn_W_up, dtype=np.float32)
    wd = np.asarray(L0_ffn_W_down, dtype=np.float32)

    wg_p = np.zeros((F_PAD, D), np.float32); wg_p[:F_REAL] = wg
    wu_p = np.zeros((F_PAD, D), np.float32); wu_p[:F_REAL] = wu
    wd_p = np.zeros((D, F_PAD), np.float32); wd_p[:, :F_REAL] = wd
    wq_h = [np.ascontiguousarray(wq[:, i * 2048:(i + 1) * 2048]) for i in range(2)]
    wk_h = [np.ascontiguousarray(wk[:, i * 512:(i + 1) * 512]) for i in range(2)]
    wv_h = [np.ascontiguousarray(wv[:, i * 512:(i + 1) * 512]) for i in range(2)]

    cos, sin = _rope_tables()
    msks = _masks()
    rtm = _rotmat()
    idb = np.eye(128, dtype=BF)

    if not _NC_CACHE:
        _NC_CACHE.append(build_nc())
    nc = _NC_CACHE[0]

    in_maps = []
    for c in range(8):
        b, h = c >> 1, c & 1
        in_maps.append({
            "x": x[b], "xown": np.ascontiguousarray(x[b, h * 2048:(h + 1) * 2048]),
            "wq": wq_h[h], "wk": wk_h[h], "wv": wv_h[h], "wo": wo,
            "wgf": wg_p, "wuf": wu_p, "wdf": wd_p,
            "costab": cos, "sintab": sin, "masks": msks,
            "identb": idb, "rotmat": rtm,
        })
    res = run_bass_kernel_spmd(nc, in_maps, core_ids=list(range(8)))
    out = np.empty((4, T, D), np.float32)
    for c in range(8):
        b, h = c >> 1, c & 1
        out[b, h * 2048:(h + 1) * 2048, :] = res.results[c]["out"]
    return out
